# version 12
# baseline (speedup 1.0000x reference)
import contextlib

import numpy as np
import ml_dtypes

import concourse.bass as bass
import concourse.mybir as mybir
from concourse.alu_op_type import AluOpType
from concourse.bass_utils import run_bass_kernel_spmd
from concourse.aot_env import allow_ici_all_to_all

F32 = mybir.dt.float32
BF16 = mybir.dt.bfloat16
AF = mybir.ActivationFunctionType
AX = mybir.AxisListType.X
OP = AluOpType

B, T, C = 16, 2048, 1024
NH, HD = 16, 64
H2, HD2 = 4, 256
E, FF, CAPACITY = 8, 4096, 4096
NCORES = 8
SEQ = 2
NSL = 256
NTOK2 = B * NSL
EPS = 1e-5
NEGB = -3.0

DBG_PHASE = 0

ENGS = ("sync", "gpsimd", "tensor", "vector", "scalar")
SEM_NAMES = (
    ["T", "V", "Sc", "G", "CC", "Cdbg"]
    + ["Cc", "Ccm", "Cw1p0", "Cw1p1", "Cw1T", "Cb1s", "Cwc", "Cwp"]
    + [f"Cx{i}" for i in range(4)] + [f"Ct{i}" for i in range(4)]
    + [f"Cw{i}" for i in range(4)] + ["Cwv0", "Cwv1"] + [f"Cxr{i}" for i in range(4)]
    + ["Cx1s0", "Cx1s1", "Cof"]
    + ["Cwsp0", "Cwsp1", "Cwtt0", "Cwtt1", "Cwisp0", "Cwisp1", "Cct", "Cze0", "Cze1",
       "Cxb0", "Cxb1", "Cxb2", "Cxb3", "Cgt0", "Cgt1", "Cyh0", "Cyh1", "Cyl0", "Cyl1", "Cot0", "Cot1",
       "Cx1r0", "Cx1r1", "Cout0", "Cout1", "Ccg"]
)


class Plan:
    def __init__(self, counts=None):
        self.events = []
        self.cnt = dict(counts or {})

    def c(self, sem):
        return self.cnt.get(sem, 0)

    def ev(self, engine, fn=None, waits=(), inc=None):
        waits = tuple((s, v) for (s, v) in waits if v > 0)
        if inc is not None:
            self.cnt[inc[0]] = self.c(inc[0]) + inc[1]
        self.events.append((engine, fn, waits, inc))
        return self.c(inc[0]) if inc is not None else None

    def t(self, fn, waits=()):
        return self.ev("tensor", fn, waits, ("T", 1))

    def v(self, fn, waits=()):
        return self.ev("vector", fn, waits, ("V", 1))

    def s(self, fn, waits=()):
        return self.ev("scalar", fn, waits, ("Sc", 1))

    def g(self, fn, waits=()):
        return self.ev("gpsimd", fn, waits, ("G", 1))

    def dma(self, engine, ch, fn, waits=()):
        return self.ev(engine, fn, waits, (ch, 16))


def emit(nc, blk, plan, sems, watermarks):
    for name in ENGS:
        deco = getattr(blk, name)

        def make(engine_name):
            def body(eng):
                wm = watermarks.setdefault(engine_name, {})
                for e_name, fn, waits, inc in plan.events:
                    if e_name != engine_name:
                        continue
                    for s, v in waits:
                        if wm.get(s, 0) < v:
                            eng.wait_ge(sems[s], v)
                            wm[s] = v
                    instr = fn(eng) if fn is not None else None
                    if inc is not None:
                        if instr is None:
                            instr = eng.nop()
                        instr.then_inc(sems[inc[0]], inc[1])
            return body

        deco(make(name))


def host_constants(core):
    r = np.arange(128)
    c = np.arange(128)
    tri_causal = (r[:, None] <= c[None, :]).astype(ml_dtypes.bfloat16)
    bp, npp = r % 16, r // 16
    bq, nq = c % 16, c // 16
    tri_s = ((bp[:, None] == bq[None, :]) & (npp[:, None] <= nq[None, :])
             ).astype(ml_dtypes.bfloat16)
    bd16 = ((r[:, None] // 16) == (c[None, :] // 16)).astype(ml_dtypes.bfloat16)
    cmask = np.stack([tri_causal, tri_s, bd16])
    bpat = (np.arange(16)[:, None] == (np.arange(128) % 16)[None, :]).astype(np.float32)
    bpatT = np.ascontiguousarray(bpat.T).astype(ml_dtypes.bfloat16)
    btri = (np.arange(16)[:, None] < np.arange(16)[None, :]).astype(np.float32)
    identf = np.eye(128, dtype=np.float32)
    onesbf = np.ones((1, 512), dtype=ml_dtypes.bfloat16)
    rankexcl = np.zeros((16, 8, 8), dtype=np.float32)
    rankexcl[:, :, :core] = 1.0
    iota8 = np.broadcast_to(np.arange(8, dtype=np.float32), (128, 8)).copy()
    return dict(cmask=cmask, bpatf=bpat, bpatT=bpatT, btri16=btri, identf=identf,
                onesbf=onesbf, rankexcl=rankexcl, iota8=iota8)


def build_nc():
    nc = bass.Bass("TRN2", target_bir_lowering=False)

    A = {}
    def din(name, shape, dt=F32):
        A[name] = nc.dram_tensor(name, shape, dt, kind="ExternalInput").ap()
    din("x", [SEQ, T, C]); din("w_attn", [C, 3 * C]); din("b_attn", [3 * C])
    din("w_proj", [C, C]); din("b_proj", [C])
    din("in_w", [3 * C, C]); din("in_b", [3 * C]); din("out_w", [C, C]); din("out_b", [C])
    din("ln1_g", [C]); din("ln1_b", [C]); din("ln2_g", [C]); din("ln2_b", [C])
    din("wg", [C, E]); din("w1_own", [C, FF]); din("b1_own", [FF])
    din("w2_own", [FF, C]); din("b2_own", [C])
    din("cmask", [3, 128, 128], BF16); din("bpatf", [16, 128]); din("bpatT", [128, 16], BF16)
    din("btri16", [16, 16])
    din("identf", [128, 128]); din("onesbf", [1, 512], BF16)
    din("rankexcl", [16, 8, 8]); din("iota8", [128, 8])

    out_ap = nc.dram_tensor("out", [B, NSL, C], F32, kind="ExternalOutput").ap()
    if DBG_PHASE == 1:
        A["dbg_x1"] = nc.dram_tensor("dbg_x1", [8, SEQ, NSL, C], BF16, kind="ExternalOutput").ap()
    if DBG_PHASE == 2:
        A["dbg_y1f"] = nc.dram_tensor("dbg_y1f", [128, 8, NTOK2], BF16, kind="ExternalOutput").ap()
        A["dbg_cnts"] = nc.dram_tensor("dbg_cnts", [128, 32, 8], F32, kind="ExternalOutput").ap()
        A["dbg_gval"] = nc.dram_tensor("dbg_gval", [128, 32], F32, kind="ExternalOutput").ap()
        A["dbg_ohb"] = nc.dram_tensor("dbg_ohb", [128, 32, 8], BF16, kind="ExternalOutput").ap()
        A["dbg_coefm"] = nc.dram_tensor("dbg_coefm", [128, 32, 8], F32, kind="ExternalOutput").ap()
        A["dbg_offsel"] = nc.dram_tensor("dbg_offsel", [128, 8], F32, kind="ExternalOutput").ap()
        A["dbg_a2o"] = nc.dram_tensor("dbg_a2o", [8, SEQ, NSL, C], F32, kind="ExternalOutput").ap()
        A["dbg_wit"] = nc.dram_tensor("dbg_wit", [128, 8, 3 * C], BF16, kind="ExternalOutput").ap()
        A["dbg_wot"] = nc.dram_tensor("dbg_wot", [128, 8, C], BF16, kind="ExternalOutput").ap()
        A["dbg_q2f"] = nc.dram_tensor("dbg_q2f", [128, 8, 512], BF16, kind="ExternalOutput").ap()
        A["dbg_x1f"] = nc.dram_tensor("dbg_x1f", [128, 8, 512], BF16, kind="ExternalOutput").ap()
        A["dbg_gf"] = nc.dram_tensor("dbg_gf", [128, 16], F32, kind="ExternalOutput").ap()
        A["dbg_wgh"] = nc.dram_tensor("dbg_wgh", [128, 8, 8], BF16, kind="ExternalOutput").ap()

    wc_in = nc.dram_tensor("wc_in", [C, C], BF16)
    wcs = nc.dram_tensor("wcs", [E, C, C], BF16)
    bias_me = nc.dram_tensor("bias_me", [C], F32)
    bias_ag = nc.dram_tensor("bias_ag", [E, C], F32)
    a2a_in = nc.dram_tensor("a2a_in", [8, SEQ, NSL, C], BF16)
    a2a_out = nc.dram_tensor("a2a_out", [8, SEQ, NSL, C], BF16)
    cnt_in = nc.dram_tensor("cnt_in", [16, 8], F32)
    cnt_ag = nc.dram_tensor("cnt_ag", [8, 16, 8], F32)
    wiT_sp = nc.dram_tensor("wiT_sp", [128, 8, 3 * C], BF16)
    drs = dict(wc_in=wc_in, wcs=wcs, bias_me=bias_me, bias_ag=bias_ag,
               a2a_in=a2a_in, a2a_out=a2a_out, cnt_in=cnt_in, cnt_ag=cnt_ag,
               wiT_sp=wiT_sp, out=out_ap)

    M = {}

    with contextlib.ExitStack() as top:
        sems = {s: top.enter_context(nc.semaphore(s)) for s in SEM_NAMES}
        blk = top.enter_context(nc.Block())
        wmarks = {}

        with contextlib.ExitStack() as S0:
            t_ = lambda *a: S0.enter_context(nc.sbuf_tensor(*a))
            cst = {}
            cst["msk"] = t_("msk", [128, 3, 128], BF16)
            cst["ones"] = t_("ones", [1, 512], BF16)
            cst["negb"] = t_("negb", [128, 1], F32)
            cst["eps"] = t_("eps", [128, 1], F32)

            with contextlib.ExitStack() as SC:
                tc = lambda *a: SC.enter_context(nc.sbuf_tensor(*a))
                cv = {}
                cv["w1p"] = tc("w1p", [128, 2, FF], BF16)
                cv["w1T"] = tc("w1T", [128, 32, C], BF16)
                cv["w2s"] = tc("w2s", [128, 32, C], BF16)
                cv["b1s"] = tc("b1s", [128, 32], BF16)
                cv["b1tmp"] = tc("b1tmp", [32, 128], BF16)
                cv["b2s"] = tc("b2s", [1, C], F32)
                cv["wcev"] = tc("wcev", [128, 2, 512], BF16)
                cv["bev"] = tc("bev", [1, 2, 512], F32)
                cv["ps_c"] = [SC.enter_context(nc.psum_tensor(f"ps_c{i}", [128, 512], F32)) for i in range(2)]
                cv["ps_cb"] = SC.enter_context(nc.psum_tensor("ps_cb", [1, 512], F32))

                p = Plan()
                gen_consts(p, A, cst, M)
                gen_combine(p, A, cst, cv, drs, M)
                emit(nc, blk, p, sems, wmarks)
                counts = p.cnt

            with contextlib.ExitStack() as S1:
                t1 = lambda *a: S1.enter_context(nc.sbuf_tensor(*a))
                bv = {}
                bv["xbf"] = t1("xbf", [128, 4, C], BF16)
                bv["x_f"] = t1("x_f", [128, 8, T], BF16)
                bv["k_f"] = t1("k_f", [128, 8, T], BF16)
                bv["v_t"] = t1("v_t", [128, 16, 16, 65], BF16)
                bv["q_fb"] = t1("q_fb", [128, 8, 512], BF16)
                bv["wbuf"] = t1("wbuf", [128, 4, 8, 128], BF16)
                bv["wv"] = t1("wv", [128, 2, 8, 512], BF16)
                bv["wp"] = t1("wp", [128, 8, C], BF16)
                bv["pT"] = t1("pT", [128, 2, 512], BF16)
                bv["o_tok"] = t1("o_tok", [128, 4, 16, 64], BF16)
                bv["o_f"] = t1("o_f", [128, 8, 512], BF16)
                bv["linv"] = t1("linv", [128, 16, 4], F32)
                bv["xres"] = t1("xres", [128, 4, C], F32)
                bv["x1b"] = t1("x1b", [128, 2, C], BF16)
                bv["s1"] = t1("s1", [128, 4], F32)
                bv["s2"] = t1("s2", [128, 4], F32)
                bv["mean"] = t1("mean", [128, 4], F32)
                bv["m2t"] = t1("m2t", [128, 4], F32)
                bv["var"] = t1("var", [128, 4], F32)
                bv["junk"] = t1("junk", [128, C], BF16)
                bv["rstd"] = t1("rstd", [128, 4], F32)
                bv["batt"] = t1("batt", [1, 3 * C], BF16)
                bv["bproj"] = t1("bprojs", [1, C], BF16)
                bv["ps_mm"] = [S1.enter_context(nc.psum_tensor(f"ps_mm{i}", [128, 512], F32)) for i in range(2)]
                bv["ps_s"] = [S1.enter_context(nc.psum_tensor(f"ps_s{i}", [128, 512], F32)) for i in range(2)]
                bv["ps_o"] = [S1.enter_context(nc.psum_tensor(f"ps_o{i}", [128, 512], F32)) for i in range(4)]

                p = Plan(counts)
                gen_phase1(p, A, cst, bv, drs, M)
                if DBG_PHASE == 1:
                    allw = ([("Cx1s0", p.c("Cx1s0")), ("Cx1s1", p.c("Cx1s1")),
                             ("T", p.c("T")), ("V", p.c("V")), ("Sc", p.c("Sc"))])
                    p.dma("gpsimd", "Cdbg", lambda g: g.dma_start(
                        out=A["dbg_x1"], in_=drs["a2a_in"].ap()), waits=allw)
                    p.ev("gpsimd", None, waits=[("Cdbg", p.c("Cdbg"))])
                else:
                    p.ev("gpsimd", lambda g: g.collective_compute(
                        "AllToAll", OP.bypass, replica_groups=[list(range(NCORES))],
                        ins=[drs["a2a_in"].ap().opt()], outs=[drs["a2a_out"].ap().opt()]),
                        waits=[("Cx1s0", p.c("Cx1s0")), ("Cx1s1", p.c("Cx1s1"))],
                        inc=("CC", 1))
                M["a2a_done"] = p.c("CC")
                emit(nc, blk, p, sems, wmarks)
                counts = p.cnt

            with contextlib.ExitStack() as S2o:
                t2o = lambda *a: S2o.enter_context(nc.sbuf_tensor(*a))
                c2 = {}
                c2["y1f"] = t2o("y1f", [128, 8, NTOK2], BF16)
                c2["cnts"] = t2o("cnts", [128, 32, 8], F32)
                c2["gval"] = t2o("gval", [128, 32], F32)
                c2["ohb"] = t2o("ohb", [128, 32, 8], BF16)
                c2["coefm"] = t2o("coefm", [128, 32, 8], F32)
                c2["carry"] = t2o("carry", [16, 8], F32)
                c2["offsel"] = t2o("offsel", [128, 8], F32)
                c2["cw"] = t2o("cw", [16, 8, 8], F32)
                c2["cw2"] = t2o("cw2", [16, 8, 8], F32)
                c2["rk16"] = t2o("rk16", [16, 8], F32)
                c2["tot16"] = t2o("tot16", [16, 8], F32)
                c2["off16"] = t2o("off16", [16, 8], F32)
                c2["bpat"] = t2o("bpat2", [16, 128], F32)
                c2["bpatT"] = t2o("bpatT2", [128, 16], BF16)
                c2["btri"] = t2o("btri2", [16, 16], F32)
                c2["idf"] = t2o("idf2", [128, 128], F32)
                c2["rke"] = t2o("rke2", [16, 8, 8], F32)
                c2["iota"] = t2o("iota2", [128, 8], F32)
                c2["gf"] = t2o("gf2", [128, 16], F32)
                c2["bfb"] = t2o("bfb2", [128, 16], BF16)
                c2["gtmp"] = t2o("gtmp", [16, 128], F32)
                c2["gtmpb"] = t2o("gtmpb", [16, 128], BF16)
                c2["g2b"] = t2o("g2b2", [128, C], BF16)
                c2["b2b"] = t2o("b2b2", [128, C], BF16)
                c2["wgf"] = t2o("wgf2", [128, 8, E], F32)
                c2["wgp"] = t2o("wgp2", [128, 8, E], BF16)
                c2["wgh"] = t2o("wgh2", [128, 8, E], BF16)
                c2["wgl"] = t2o("wgl2", [128, 8, E], BF16)
                c2["bwgb"] = t2o("bwgb", [1, E], BF16)
                c2["biasm"] = t2o("biasm2", [9, C], BF16)
                c2["inb"] = t2o("inb2", [1, 3 * C], BF16)
                c2["outb"] = t2o("outb2", [1, C], BF16)
                c2["s1b"] = t2o("s1b", [128, 4], F32)
                c2["s2b"] = t2o("s2b", [128, 4], F32)
                c2["meanb"] = t2o("meanb", [128, 4], F32)
                c2["m2tb"] = t2o("m2tb", [128, 4], F32)
                c2["varb"] = t2o("varb", [128, 4], F32)
                c2["rstdb"] = t2o("rstdb", [128, 4], F32)
                c2["junk2"] = t2o("junk2", [128, C], BF16)

                with contextlib.ExitStack() as S2a:
                    t2a = lambda *a: S2a.enter_context(nc.sbuf_tensor(*a))
                    av = {}
                    av["woT"] = t2a("woT", [128, 8, C], BF16)
                    av["ps2_mm"] = [S2a.enter_context(nc.psum_tensor(f"ps2mm{i}", [128, 512], F32)) for i in range(2)]
                    av["ps2_s"] = [S2a.enter_context(nc.psum_tensor(f"ps2s{i}", [128, 512], F32)) for i in range(2)]
                    av["ps2_o"] = [S2a.enter_context(nc.psum_tensor(f"ps2o{i}", [128, 512], F32)) for i in range(4)]
                    with contextlib.ExitStack() as S2s:
                        av["wtmp"] = S2s.enter_context(nc.sbuf_tensor("wtmp", [128, 2, C], BF16))
                        av["wiTt"] = S2s.enter_context(nc.sbuf_tensor("wiTt", [128, 2, 8, 128], BF16))
                        p = Plan(counts)
                        gen_phase2a_prep(p, A, cst, c2, av, drs, M)
                        emit(nc, blk, p, sems, wmarks)
                        counts = p.cnt
                    av["wibuf"] = t2a("wibuf", [128, 2, 8, 128], BF16)
                    av["wvbuf"] = t2a("wvbuf", [128, 1, 8, 512], BF16)
                    av["x1bf"] = t2a("x1bf", [128, 4, C], BF16)
                    av["x1f"] = t2a("x1f", [128, 8, 512], BF16)
                    av["q2f"] = t2a("q2f", [128, 8, 512], BF16)
                    av["k2f"] = t2a("k2f", [128, 8, 512], BF16)
                    av["v2t"] = t2a("v2t", [128, 4, 4, 257], BF16)
                    av["pT2"] = t2a("pT2", [128, 2, 128], BF16)
                    av["linv2"] = t2a("linv2", [128, 4], F32)
                    av["o2tok"] = t2a("o2tok", [128, 2, C], BF16)
                    av["o2f"] = t2a("o2f", [128, 8, 512], BF16)
                    av["xres2"] = t2a("xres2", [128, 4, C], F32)
                    av["xrebf"] = t2a("xrebf", [128, 4, C], BF16)
                    av["htok"] = t2a("htok", [128, 2, C], BF16)
                    av["y1lo"] = av["k2f"]
                    av["gt4"] = t2a("gt4", [128, 4, 8], F32)
                    av["rt1"] = t2a("rt1", [128, 4, 8], F32)
                    av["rt2"] = t2a("rt2", [128, 4, 8], F32)
                    av["mx4"] = t2a("mx4", [128, 4], F32)
                    av["sm4"] = t2a("sm4", [128, 4], F32)
                    av["i9"] = t2a("i9", [128, 8], F32)
                    av["hfull"] = t2a("hfull", [128, C], F32)
                    av["lotok"] = t2a("lotok", [128, 2, C], BF16)
                    p = Plan(counts)
                    gen_phase2a(p, A, cst, c2, av, drs, M)
                    emit(nc, blk, p, sems, wmarks)
                    counts = p.cnt

                with contextlib.ExitStack() as S2b:
                    t2b = lambda *a: S2b.enter_context(nc.sbuf_tensor(*a))
                    bv2 = {}
                    bv2["wc"] = t2b("wc", [128, 2, 8, C], BF16)
                    c2["coefs"] = t2b("coefs", [9, NTOK2], BF16)
                    bv2["rtb"] = t2b("rtb", [128, 8], F32)
                    bv2["ztile"] = t2b("ztile", [128, 8, C], F32)
                    bv2["y1tok"] = t2b("y1tok", [128, 2, C], BF16)
                    bv2["h2"] = t2b("h2", [128, 4, C], F32)
                    bv2["x1rl"] = t2b("x1rl", [128, 2, C], BF16)
                    bv2["outtok"] = t2b("outtok", [128, 2, C], F32)
                    bv2["zps"] = [S2b.enter_context(nc.psum_tensor(f"zps{i}", [128, 512], F32)) for i in range(4)]

                    p = Plan(counts)
                    gen_route_finish(p, A, cst, c2, bv2, drs, M)
                    gen_phase2bc(p, A, cst, c2, bv2, drs, M)
                    emit(nc, blk, p, sems, wmarks)
                    counts = p.cnt

    return nc


def gen_consts(p, A, cst, M):
    ch = "Cc"
    p.dma("gpsimd", ch, lambda g: g.dma_start(out=cst["msk"][:, :, :], in_=A["cmask"].transpose([1, 0, 2])))
    p.dma("gpsimd", ch, lambda g: g.dma_start(out=cst["ones"][:, :], in_=A["onesbf"]))
    M["consts"] = p.c(ch)
    p.v(lambda v: v.memset(cst["negb"][:, :], NEGB))
    M["negb"] = p.v(lambda v: v.memset(cst["eps"][:, :], EPS))


def gen_combine(p, A, cst, cv, drs, M):
    w1p, w1T, w2s = cv["w1p"], cv["w1T"], cv["w2s"]
    b1s, b2s, wcev, bev = cv["b1s"], cv["b2s"], cv["wcev"], cv["bev"]
    ps_c, ps_cb = cv["ps_c"], cv["ps_cb"]

    p.dma("gpsimd", "Ccm", lambda g: g.dma_start(
        out=w2s[:, :, :], in_=A["w2_own"].rearrange("(f p) d -> p f d", p=128)))
    p.dma("gpsimd", "Ccm", lambda g: g.dma_start(
        out=cv["b1tmp"][:, :], in_=A["b1_own"].rearrange("(f p) -> f p", p=128)))
    p.dma("gpsimd", "Ccm", lambda g: g.dma_start(out=b2s[:, :], in_=A["b2_own"].unsqueeze(0)))
    mCM = p.c("Ccm")
    p.dma("sync", "Cb1s", lambda s: s.dma_start(
        out=b1s[:, :], in_=cv["b1tmp"][:, :], transpose=True), waits=[("Ccm", mCM)])

    tr_mark = {}
    for cc in range(8):
        chp = f"Cw1p{cc % 2}"
        mld = p.dma("gpsimd", chp, lambda g, cc=cc: g.dma_start(
            out=w1p[:, cc % 2, :], in_=A["w1_own"].rearrange("(c p) f -> p c f", p=128)[:, cc, :]),
            waits=([("Cw1T", tr_mark[cc - 2])] if cc >= 2 else []))
        for f in range(32):
            p.dma("sync", "Cw1T", lambda s, cc=cc, f=f: s.dma_start(
                out=w1T[:, f, 128 * cc:128 * (cc + 1)],
                in_=w1p[:, cc % 2, 128 * f:128 * (f + 1)], transpose=True),
                waits=[(chp, mld)])
        tr_mark[cc] = p.c("Cw1T")
    mW1T = p.c("Cw1T")

    ev_mark = [0, 0]
    for di in range(8):
        for ch in range(2):
            pb = (di * 2 + ch) % 2
            for f in range(32):
                w = [("Cw1T", mW1T), ("Ccm", mCM), ("V", ev_mark[pb])] if f == 0 else []
                p.t(lambda t, di=di, ch=ch, f=f, pb=pb: t.matmul(
                    ps_c[pb][:, :], w1T[:, f, 128 * di:128 * (di + 1)],
                    w2s[:, f, 512 * ch:512 * (ch + 1)], start=(f == 0), stop=(f == 31)),
                    waits=w)
            mC = p.c("T")
            ev_mark[pb] = p.v(lambda v, pb=pb: v.tensor_copy(wcev[:, pb, :], ps_c[pb][:, :]),
                              waits=[("T", mC)])
            p.dma("gpsimd", "Cwc", lambda g, di=di, ch=ch, pb=pb: g.dma_start(
                out=drs["wc_in"].ap()[128 * di:128 * (di + 1), 512 * ch:512 * (ch + 1)],
                in_=wcev[:, pb, :]), waits=[("V", ev_mark[pb])])
    for ch in range(2):
        for f in range(32):
            w = [("V", p.c("V")), ("Ccm", mCM), ("Cb1s", 16)] if f == 0 else []
            p.t(lambda t, ch=ch, f=f: t.matmul(
                ps_cb[:, :], b1s[:, f:f + 1], w2s[:, f, 512 * ch:512 * (ch + 1)],
                start=(f == 0), stop=(f == 31)), waits=w)
        mC = p.c("T")
        mv = p.v(lambda v, ch=ch: v.tensor_tensor(
            bev[:, ch, :], ps_cb[:, :], b2s[:, 512 * ch:512 * (ch + 1)], op=OP.add),
            waits=[("T", mC)])
        p.dma("gpsimd", "Cwc", lambda g, ch=ch: g.dma_start(
            out=drs["bias_me"].ap()[512 * ch:512 * (ch + 1)].unsqueeze(0), in_=bev[:, ch, :]),
            waits=[("V", mv)])
    mWC = p.c("Cwc")
    p.ev("gpsimd", lambda g: g.collective_compute(
        "AllGather", OP.bypass, replica_groups=[list(range(NCORES))],
        ins=[drs["wc_in"].ap().opt()], outs=[drs["wcs"].ap().opt()]),
        waits=[("Cwc", mWC)], inc=("CC", 1))
    p.ev("gpsimd", lambda g: g.collective_compute(
        "AllGather", OP.bypass, replica_groups=[list(range(NCORES))],
        ins=[drs["bias_me"].ap().opt()], outs=[drs["bias_ag"].ap().opt()]), inc=("CC", 1))
    M["ag_done"] = p.c("CC")


def gen_phase1(p, A, cst, bv, drs, M):
    x_f, k_f, v_t, q_fb = bv["x_f"], bv["k_f"], bv["v_t"], bv["q_fb"]
    xbf, wbuf, wv, wp = bv["xbf"], bv["wbuf"], bv["wv"], bv["wp"]
    pT, o_tok, o_f, linv = bv["pT"], bv["o_tok"], bv["o_f"], bv["linv"]
    xres, x1b, rstd = bv["xres"], bv["x1b"], bv["rstd"]
    s1, s2, mean, m2t, var, junk = bv["s1"], bv["s2"], bv["mean"], bv["m2t"], bv["var"], bv["junk"]
    ps_mm, ps_s, ps_o = bv["ps_mm"], bv["ps_s"], bv["ps_o"]
    ones, negb, eps = cst["ones"], cst["negb"], cst["eps"]
    batt, bproj = bv["batt"], bv["bproj"]
    tri = cst["msk"][:, 0, :]
    a2a_in = drs["a2a_in"]

    p.dma("gpsimd", "Cc", lambda g: g.dma_start(out=batt[:, :], in_=A["b_attn"].unsqueeze(0)))
    p.dma("gpsimd", "Cc", lambda g: g.dma_start(out=bproj[:, :], in_=A["b_proj"].unsqueeze(0)))
    M["consts"] = p.c("Cc")
    mWP = p.dma("gpsimd", "Cwp", lambda g: g.dma_start(
        out=wp[:, :, :], in_=A["w_proj"].rearrange("(k p) d -> p k d", p=128)))

    w_free = {}
    ev_psmm = [("V", 0), ("V", 0)]
    v_pso = [0] * 4
    qev = {}
    ss_free = [0, 0]
    pT_free = [0, 0]
    oe_last = [0] * 4
    linv_rd = {}
    of_cnt = [0]
    xr_free = [0, 0, 0, 0]
    x1_cnt = [0, 0]
    sq_last = [0]
    stat_rd = [0]
    widx = [0]
    xf_free = [0]
    qk_prev = [0]
    proj_prev = [0]
    vpass_done = [0]

    def wchunk_load(dk):
        i = widx[0]
        ch = f"Cw{i % 4}"
        m = p.dma("gpsimd", ch, lambda g, dk=dk, i=i: g.dma_start(
            out=wbuf[:, i % 4, :, :],
            in_=A["w_attn"][:, 128 * dk:128 * (dk + 1)].rearrange("(k p) f -> p k f", p=128)),
            waits=[("T", w_free.get(i - 4, 0))])
        widx[0] += 1
        return i, ch, m

    def big_mm(pb, lhs_list, rhs_list, bias_pair, extra_waits):
        for i in range(len(lhs_list)):
            w = (list(extra_waits) + [ev_psmm[pb]]) if i == 0 else []
            st = (i == 0)
            sp = (i == len(lhs_list) - 1 and bias_pair is None)
            p.t(lambda t, i=i, st=st, sp=sp, pb=pb: t.matmul(
                ps_mm[pb][:, :], lhs_list[i](), rhs_list[i](), start=st, stop=sp), waits=w)
        if bias_pair is not None:
            lf, rf = bias_pair
            p.t(lambda t, pb=pb: t.matmul(ps_mm[pb][:, :], lf(), rf(), start=False, stop=True),
                waits=[("Cc", M["consts"])])
        return p.c("T")

    for s in range(SEQ):
        for tl in range(16):
            gt16 = 16 * s + tl
            slot = gt16 % 4
            waits = []
            if gt16 >= 4:
                waits.append((f"Ct{slot}", 128 * (gt16 // 4)))
            if s == 1 and tl < 4:
                waits.append(("T", xf_free[0]))
            mxl = p.dma("gpsimd", f"Cx{slot}", lambda g, s=s, tl=tl, slot=slot: g.dma_start(
                out=xbf[:, slot, :], in_=A["x"][s, 128 * tl:128 * (tl + 1), :]), waits=waits)
            for j in range(8):
                p.dma("sync", f"Ct{slot}", lambda y, tl=tl, j=j, slot=slot: y.dma_start(
                    out=x_f[:, j, 128 * tl:128 * (tl + 1)],
                    in_=xbf[:, slot, 128 * j:128 * (j + 1)], transpose=True),
                    waits=[(f"Cx{slot}", mxl)])

        def xf_wait(tl, s=s):
            gt16 = 16 * s + tl
            return (f"Ct{gt16 % 4}", 128 * (gt16 // 4 + 1))

        for dk8 in range(8):
            wi, ch, wm = wchunk_load(8 + dk8)
            for tb in range(4):
                pb = (dk8 * 4 + tb) % 2
                mm = big_mm(
                    pb,
                    [(lambda cc=cc, wi=wi: wbuf[:, wi % 4, cc, :]) for cc in range(8)],
                    [(lambda cc=cc, tb=tb: x_f[:, cc, 512 * tb:512 * (tb + 1)]) for cc in range(8)],
                    ((lambda dk8=dk8: batt[0:1, 1024 + 128 * dk8:1024 + 128 * (dk8 + 1)]),
                     (lambda: ones[0:1, 0:512])),
                    [(ch, wm)] + [xf_wait(4 * tb + jj) for jj in range(4)])
                ev_psmm[pb] = ("V", p.v(lambda v, dk8=dk8, tb=tb, pb=pb: v.tensor_copy(
                    k_f[:, dk8, 512 * tb:512 * (tb + 1)], ps_mm[pb][:, :]), waits=[("T", mm)]))
            w_free[wi] = mm

        mwv = []
        for ch_ in range(2):
            mwv.append(p.dma("gpsimd", f"Cwv{ch_}", lambda g, ch_=ch_: g.dma_start(
                out=wv[:, ch_, :, :],
                in_=A["w_attn"][:, 2 * C + 512 * ch_:2 * C + 512 * (ch_ + 1)]
                .rearrange("(k p) f -> p k f", p=128)),
                waits=[("T", vpass_done[0])]))
        if s == 0:
            p.v(lambda v: v.memset(v_t[:, :, :, 64:65], 1.0))
        mones = p.c("V")
        for tt in range(16):
            for ch_ in range(2):
                pb = (tt * 2 + ch_) % 2
                mm = big_mm(
                    pb,
                    [(lambda cc=cc, tt=tt: x_f[:, cc, 128 * tt:128 * (tt + 1)]) for cc in range(8)],
                    [(lambda cc=cc, ch_=ch_: wv[:, ch_, cc, :]) for cc in range(8)],
                    ((lambda: ones[0:1, 0:128]),
                     (lambda ch_=ch_: batt[0:1, 2 * C + 512 * ch_:2 * C + 512 * (ch_ + 1)])),
                    [(f"Cwv{ch_}", mwv[ch_]), xf_wait(tt), ("V", mones)])
                ev_psmm[pb] = ("V", p.v(lambda v, tt=tt, ch_=ch_, pb=pb: v.tensor_copy(
                    v_t[:, tt, 8 * ch_:8 * (ch_ + 1), 0:64], ps_mm[pb][:, :]), waits=[("T", mm)]))
        vpass_done[0] = p.c("T")

        qpb = [ps_mm[0], ps_mm[1]] + [ps_o[i] for i in range(4)]

        def qpass(q, defer):
            if not defer:
                for dq in range(8):
                    wi, ch, wm = wchunk_load(dq)
                    pb = dq % 2
                    mm = big_mm(
                        pb,
                        [(lambda cc=cc, wi=wi: wbuf[:, wi % 4, cc, :]) for cc in range(8)],
                        [(lambda cc=cc, q=q: x_f[:, cc, 512 * q:512 * (q + 1)]) for cc in range(8)],
                        ((lambda dq=dq: batt[0:1, 128 * dq:128 * (dq + 1)]),
                         (lambda: ones[0:1, 0:512])),
                        [(ch, wm)])
                    w_free[wi] = mm
                    ev_psmm[pb] = ("V", p.v(lambda v, dq=dq, pb=pb: v.tensor_copy(
                        q_fb[:, dq, :], ps_mm[pb][:, :]),
                        waits=[("T", mm), ("T", qk_prev[0])]))
                qev[q] = ("V", p.c("V"))
                return
            mmk = {}

            def qmm(dq):
                wi, ch, wm = wchunk_load(dq)
                j = dq % 6
                bank = qpb[j]
                w0 = [ev_psmm[j]] if j < 2 else \
                    [("Sc", oe_last[j - 2]), ("V", v_pso[j - 2])]
                for cc in range(8):
                    p.t(lambda t, cc=cc, wi=wi, bank=bank, q=q: t.matmul(
                        bank[:, :], wbuf[:, wi % 4, cc, :],
                        x_f[:, cc, 512 * q:512 * (q + 1)],
                        start=(cc == 0), stop=False),
                        waits=(w0 + [(ch, wm)]) if cc == 0 else [])
                p.t(lambda t, dq=dq, bank=bank: t.matmul(
                    bank[:, :], batt[0:1, 128 * dq:128 * (dq + 1)], ones[0:1, 0:512],
                    start=False, stop=True), waits=[("Cc", M["consts"])])
                mm = p.c("T")
                w_free[wi] = mm
                mmk[dq] = mm

            def qevac(dq):
                j = dq % 6
                bank = qpb[j]
                m = p.s(lambda sc, dq=dq, bank=bank: sc.activation(
                    q_fb[:, dq, :], bank[:, :], AF.Copy),
                    waits=[("T", mmk[dq]), ("T", qk_prev[0])])
                if j < 2:
                    ev_psmm[j] = ("Sc", m)
                else:
                    oe_last[j - 2] = m
                return m

            for dq in range(6):
                qmm(dq)

            def rest():
                qevac(0)
                qmm(6)
                qevac(1)
                qmm(7)
                m = 0
                for dq in range(2, 8):
                    m = qevac(dq)
                qev[q] = ("Sc", m)
            qrest[0] = rest

        qrest = [None]
        for qb in range(4):
            if qb == 0:
                qpass(0, defer=False)

            nkt = 4 * qb + 4
            for h in range(NH):
                po = 64 * (h % 2)
                hc = h // 2

                def emit_pv(kt, me, h=h, qb=qb):
                    rel = kt - 4 * qb
                    wpv = [("Sc", me)]
                    if rel >= 0:
                        mmk = p.v(lambda v, kt=kt, rel=rel: v.tensor_tensor(
                            pT[:, kt % 2, 128 * rel:128 * (rel + 1)],
                            pT[:, kt % 2, 128 * rel:128 * (rel + 1)], tri, op=OP.mult),
                            waits=[("Sc", me), ("Cc", M["consts"])])
                        wpv.append(("V", mmk))
                    for qt in range(max(0, rel), 4):
                        if kt == 0:
                            wpv.append(("Sc", oe_last[qt]))
                        p.t(lambda t, kt=kt, qt=qt, h=h, qb=qb: t.matmul(
                            ps_o[qt][:, 0:65], pT[:, kt % 2, 128 * qt:128 * (qt + 1)],
                            v_t[:, kt, h, :], start=(kt == 0), stop=(kt == 4 * qb + qt)),
                            waits=wpv)
                        wpv = []
                    pT_free[kt % 2] = p.c("T")

                pend = None
                for kt in range(nkt):
                    p.t(lambda t, po=po, hc=hc, kt=kt: t.matmul(
                        ps_s[kt % 2][:, :], k_f[po:po + 64, hc, 128 * kt:128 * (kt + 1)],
                        q_fb[po:po + 64, hc, :], start=True, stop=True),
                        waits=[("Sc", ss_free[kt % 2]), qev[qb]])
                    mqk = p.c("T")
                    vs = max(0, kt - 4 * qb) * 128
                    me = p.s(lambda sc, kt=kt, vs=vs: sc.activation(
                        pT[:, kt % 2, vs:512], ps_s[kt % 2][:, vs:512], AF.Exp,
                        bias=negb[:, :], scale=0.125),
                        waits=[("T", mqk), ("T", pT_free[kt % 2]), ("V", M["negb"])])
                    ss_free[kt % 2] = me
                    if pend is not None:
                        emit_pv(*pend)
                    pend = (kt, me)
                emit_pv(*pend)
                for qt in range(4):
                    mrc = p.v(lambda v, h=h, qt=qt: v.reciprocal(
                        linv[:, h, qt:qt + 1], ps_o[qt][:, 64:65]),
                        waits=[("T", pT_free[(4 * qb + qt) % 2]),
                               ("Sc", linv_rd.get((h, qt), 0))])
                    v_pso[qt] = mrc
                    oe_last[qt] = p.s(lambda sc, h=h, qt=qt: sc.activation(
                        o_tok[:, qt, h, :], ps_o[qt][:, 0:64], AF.Copy,
                        scale=linv[:, h, qt:qt + 1]),
                        waits=[("V", mrc), ("Cof", 512 * of_cnt[0])])
                    linv_rd[(h, qt)] = oe_last[qt]
            qk_prev[0] = p.c("T")
            moall = p.c("Sc")

            for qt in range(4):
                for j in range(8):
                    p.dma("sync", "Cof", lambda y, qt=qt, j=j: y.dma_start(
                        out=o_f[:, j, 128 * qt:128 * (qt + 1)],
                        in_=o_tok[:, qt, 2 * j:2 * j + 2, :].rearrange("p a b -> p (a b)"),
                        transpose=True),
                        waits=[("Sc", moall), ("T", proj_prev[0])])
            of_cnt[0] += 1
            mOF = 512 * of_cnt[0]

            for tb in range(4):
                gt = 4 * qb + tb
                chx = f"Cxr{tb}"
                mxr = p.dma("gpsimd", chx, lambda g, s=s, gt=gt, tb=tb: g.dma_start(
                    out=xres[:, tb, :], in_=A["x"][s, 128 * gt:128 * (gt + 1), :]),
                    waits=[("V", xr_free[tb])])
                for ch_ in range(2):
                    pb = ch_
                    mm = big_mm(
                        pb,
                        [(lambda cc=cc, tb=tb: o_f[:, cc, 128 * tb:128 * (tb + 1)]) for cc in range(8)],
                        [(lambda cc=cc, ch_=ch_: wp[:, cc, 512 * ch_:512 * (ch_ + 1)]) for cc in range(8)],
                        ((lambda: ones[0:1, 0:128]),
                         (lambda ch_=ch_: bproj[0:1, 512 * ch_:512 * (ch_ + 1)])),
                        [("Cof", mOF), ("Cwp", mWP)])
                    mres = p.v(lambda v, tb=tb, ch_=ch_, pb=pb: v.tensor_tensor(
                        xres[:, tb, 512 * ch_:512 * (ch_ + 1)], ps_mm[pb][:, :],
                        xres[:, tb, 512 * ch_:512 * (ch_ + 1)], op=OP.add),
                        waits=[("T", mm), (chx, mxr)])
                    ev_psmm[pb] = ("V", mres)
                p.v(lambda v, tb=tb: (v.tensor_reduce(
                    s1[:, tb:tb + 1], xres[:, tb, :], axis=AX, op=OP.add), v.drain())[-1],
                    waits=[("V", stat_rd[0])])
                sq_last[0] = p.s(lambda sc, tb=tb: (sc.activation(
                    junk[:, :], xres[:, tb, :], AF.Square,
                    accum_out=s2[:, tb:tb + 1]), sc.drain())[-1],
                    waits=[("V", mres), ("V", stat_rd[0])])
            if qb < 3:
                qpass(qb + 1, defer=True)
            mrd = p.c("V")
            p.v(lambda v: (v.tensor_scalar(mean[:, :], s1[:, :], 1.0 / C, None, op0=OP.mult),
                           v.drain())[-1])
            p.v(lambda v: (v.tensor_tensor(m2t[:, :], mean[:, :], mean[:, :], op=OP.mult),
                           v.drain())[-1])
            mm2 = p.v(lambda v: (v.scalar_tensor_tensor(
                var[:, :], s2[:, :], 1.0 / C, m2t[:, :], op0=OP.mult, op1=OP.subtract),
                v.drain())[-1], waits=[("Sc", sq_last[0])])
            mvar = p.v(lambda v: (v.tensor_scalar(var[:, :], var[:, :], EPS, None, op0=OP.add),
                                  v.drain())[-1])
            msr = p.s(lambda sc: (sc.activation(rstd[:, :], var[:, :], AF.Sqrt, scale=1.0),
                                  sc.drain())[-1],
                      waits=[("V", mvar)])
            p.v(lambda v: (v.reciprocal(rstd[:, :], rstd[:, :]), v.drain())[-1],
                waits=[("Sc", msr)])
            p.v(lambda v: (v.tensor_tensor(m2t[:, :], rstd[:, :], rstd[:, :], op=OP.mult),
                           v.drain())[-1])
            p.v(lambda v: (v.tensor_tensor(m2t[:, :], m2t[:, :], var[:, :], op=OP.mult),
                           v.drain())[-1])
            p.v(lambda v: (v.tensor_scalar(m2t[:, :], m2t[:, :], -0.5, 1.5, op0=OP.mult, op1=OP.add),
                           v.drain())[-1])
            stat_rd[0] = p.v(lambda v: (v.tensor_tensor(rstd[:, :], rstd[:, :], m2t[:, :], op=OP.mult),
                                        v.drain())[-1])
            if qb < 3:
                qrest[0]()
            for tb in range(4):
                gt = 4 * qb + tb
                chs = f"Cx1s{tb % 2}"
                mln = p.v(lambda v, tb=tb: v.tensor_scalar(
                    x1b[:, tb % 2, :], xres[:, tb, :], mean[:, tb:tb + 1], rstd[:, tb:tb + 1],
                    op0=OP.subtract, op1=OP.mult),
                    waits=[(chs, 16 * x1_cnt[tb % 2])])
                xr_free[tb] = mln
                shard, off = gt // 2, 128 * (gt % 2)
                p.dma("scalar", chs, lambda sc, shard=shard, s=s, off=off, tb=tb: sc.dma_start(
                    out=a2a_in.ap()[shard, s, off:off + 128, :], in_=x1b[:, tb % 2, :]),
                    waits=[("V", mln)])
                x1_cnt[tb % 2] += 1
            proj_prev[0] = p.c("T")
        xf_free[0] = p.c("T")




def gen_phase2a_prep(p, A, cst, c2, av, drs, M):
    ones = cst["ones"]
    wiT_sp = drs["wiT_sp"]

    def lg(fn, waits=()):
        return p.dma("gpsimd", "Ccg", fn, waits=waits)
    lg(lambda g: g.dma_start(out=c2["bpat"][:, :], in_=A["bpatf"]))
    lg(lambda g: g.dma_start(out=c2["bpatT"][:, :], in_=A["bpatT"]))
    lg(lambda g: g.dma_start(out=c2["btri"][:, :], in_=A["btri16"]))
    lg(lambda g: g.dma_start(out=c2["idf"][:, :], in_=A["identf"]))
    lg(lambda g: g.dma_start(out=c2["rke"][:, :, :], in_=A["rankexcl"]))
    lg(lambda g: g.dma_start(out=c2["iota"][:, :], in_=A["iota8"]))
    mg1 = p.dma("gpsimd", "Cgt0", lambda g: g.dma_start(
        out=c2["gtmp"][0:8, :], in_=A["ln1_g"].rearrange("(k p) -> k p", p=128)))
    p.t(lambda t: t.transpose(av["ps2_s"][1][:, 0:16], c2["gtmp"][:, :], c2["idf"][0:16, 0:16]),
        waits=[("Cgt0", mg1), ("Ccg", p.c("Ccg"))])
    mgt = p.c("T")
    M["_gfev"] = p.v(lambda v: (v.tensor_copy(c2["gf"][:, :], av["ps2_s"][1][:, 0:16]),
                                v.drain())[-1], waits=[("T", mgt)])
    mg2 = p.dma("gpsimd", "Cgt1", lambda g: g.dma_start(
        out=c2["gtmpb"][0:8, :], in_=A["ln1_b"].rearrange("(k p) -> k p", p=128)))
    p.dma("sync", "Cgt1", lambda y: y.dma_start(out=c2["bfb"][:, :], in_=c2["gtmpb"][:, :],
                                                transpose=True), waits=[("Cgt1", mg2)])
    lg(lambda g: g.dma_start(out=c2["g2b"][:, :], in_=A["ln2_g"].unsqueeze(0).broadcast_to([128, C])))
    lg(lambda g: g.dma_start(out=c2["b2b"][:, :], in_=A["ln2_b"].unsqueeze(0).broadcast_to([128, C])))
    lg(lambda g: g.dma_start(out=c2["wgf"][:, :, :], in_=A["wg"].rearrange("(k p) e -> p k e", p=128)))
    lg(lambda g: g.dma_start(out=c2["wgp"][:, :, :], in_=A["wg"].rearrange("(k p) e -> p k e", p=128)))
    lg(lambda g: g.dma_start(out=c2["inb"][:, :], in_=A["in_b"].unsqueeze(0)))
    lg(lambda g: g.dma_start(out=c2["outb"][:, :], in_=A["out_b"].unsqueeze(0)))
    lg(lambda g: g.dma_start(out=c2["biasm"][8:9, :], in_=A["ln1_b"].unsqueeze(0)))
    lg(lambda g: g.dma_start(out=c2["biasm"][0:8, :], in_=drs["bias_ag"].ap()),
       waits=[("CC", M["ag_done"])])
    mCG = p.c("Ccg")

    for cc in range(8):
        p.v(lambda v, cc=cc: v.tensor_scalar(
            c2["wgf"][:, cc, :], c2["wgf"][:, cc, :], c2["gf"][:, cc:cc + 1], None,
            op0=OP.mult), waits=([("Ccg", mCG), ("V", M["_gfev"])] if cc == 0 else ()))
    p.v(lambda v: v.drain())
    p.v(lambda v: (v.tensor_copy(c2["wgh"][:, :, :], c2["wgf"][:, :, :]), v.drain())[-1])
    p.v(lambda v: (v.scalar_tensor_tensor(
        c2["wgl"][:, :, :], c2["wgh"][:, :, :], -1.0, c2["wgf"][:, :, :],
        op0=OP.mult, op1=OP.add), v.drain())[-1])

    for cc in range(8):
        p.t(lambda t, cc=cc: t.matmul(
            av["ps2_s"][0][0:1, 0:8], c2["bfb"][:, cc:cc + 1], c2["wgp"][:, cc, :],
            start=(cc == 0), stop=(cc == 7)),
            waits=[("Ccg", mCG), ("Cgt1", p.c("Cgt1"))] if cc == 0 else ())
    mbw = p.c("T")
    M["_mbwe"] = p.v(lambda v: (v.tensor_copy(c2["bwgb"][:, :], av["ps2_s"][0][0:1, 0:8]),
                                 v.drain())[-1], waits=[("T", mbw)])

    wtt_mark = [0, 0]
    wsp_cnt = [0, 0]
    for dd in range(24):
        chl = f"Cwsp{dd % 2}"
        mld = p.dma("gpsimd", chl, lambda g, dd=dd: g.dma_start(
            out=av["wtmp"][:, dd % 2, :], in_=A["in_w"][128 * dd:128 * (dd + 1), :]),
            waits=[(f"Cwtt{dd % 2}", wtt_mark[dd % 2])])
        for j in range(8):
            p.dma("sync", f"Cwtt{dd % 2}", lambda y, dd=dd, j=j: y.dma_start(
                out=av["wiTt"][:, dd % 2, j, :],
                in_=av["wtmp"][:, dd % 2, 128 * j:128 * (j + 1)], transpose=True),
                waits=[(chl, mld), (f"Cwisp{dd % 2}", wsp_cnt[dd % 2])])
        wtt_mark[dd % 2] = p.c(f"Cwtt{dd % 2}")
        wsp_cnt[dd % 2] = p.dma("gpsimd", f"Cwisp{dd % 2}", lambda g, dd=dd: g.dma_start(
            out=wiT_sp.ap()[:, :, 128 * dd:128 * (dd + 1)], in_=av["wiTt"][:, dd % 2, :, :]),
            waits=[(f"Cwtt{dd % 2}", wtt_mark[dd % 2])])
    for dd in range(8):
        chl = f"Cwsp{dd % 2}"
        mld = p.dma("gpsimd", chl, lambda g, dd=dd: g.dma_start(
            out=av["wtmp"][:, dd % 2, :], in_=A["out_w"][128 * dd:128 * (dd + 1), :]),
            waits=[(f"Cwtt{dd % 2}", wtt_mark[dd % 2])])
        for j in range(8):
            p.dma("sync", f"Cwtt{dd % 2}", lambda y, dd=dd, j=j: y.dma_start(
                out=av["woT"][:, j, 128 * dd:128 * (dd + 1)],
                in_=av["wtmp"][:, dd % 2, 128 * j:128 * (j + 1)], transpose=True),
                waits=[(chl, mld)])
        wtt_mark[dd % 2] = p.c(f"Cwtt{dd % 2}")
    M["mWOT"] = p.c("Cwtt0")
    M["mWOT1"] = p.c("Cwtt1")
    M["mWISP"] = (p.c("Cwisp0"), p.c("Cwisp1"))
    M["mCG"] = p.c("Ccg")
    M["mbwe"] = M.get("_mbwe", 0)


def gen_phase2a(p, A, cst, c2, av, drs, M):
    ones, negb = cst["ones"], cst["negb"]
    bd16 = cst["msk"][:, 2, :]
    tri_s = cst["msk"][:, 1, :]
    y1f, cnts, gval, ohb = c2["y1f"], c2["cnts"], c2["gval"], c2["ohb"]
    carry = c2["carry"]
    wiT_sp, a2a_out = drs["wiT_sp"], drs["a2a_out"]
    mCG = M["mCG"]
    mWISP0, mWISP1 = M["mWISP"]
    mWOT, mWOT1 = M["mWOT"], M["mWOT1"]
    mbwe = M["_mbwe"]

    p.v(lambda v: (v.tensor_scalar(av["i9"][:, :], c2["iota"][:, :], 9.0, None,
                                   op0=OP.subtract), v.drain())[-1], waits=[("Ccg", mCG)])
    p.v(lambda v: v.memset(av["v2t"][:, :, :, 256:257], 1.0))
    p.v(lambda v: v.memset(av["linv2"][:, :], 1.0))
    mPRE = p.v(lambda v: (v.memset(c2["carry"][:, :], 0.0), v.drain())[-1])

    ev_ps = [("V", 0), ("V", 0)]
    ss_free = [0, 0]
    pT_free = [0, 0]
    oe_last = [0] * 4
    v_ps2o = [0] * 4
    linv_rd = {}
    wi_free = {}
    xf_mms = [0]
    xrb_free = [0, 0, 0, 0]
    xr2_free = [0, 0, 0, 0]
    proj_T = [0]
    stat_rd = [0]
    gates_T = [0]
    carryV = [mPRE]
    gsc_V = [0]
    widx = [0]
    ct_full = [p.c(f"Ct{i}") for i in range(2)]
    xload_mark = [0, 0, 0, 0]
    yh_free = [0, 0]
    yl_free = [0, 0]
    ot_free = [0, 0]
    cntev = {}
    mQKV_t = [("V", 0)]

    def big_mm2(pb, lhs_list, rhs_list, bias_pair, extra_waits):
        for i in range(len(lhs_list)):
            w = (list(extra_waits) + [ev_ps[pb]]) if i == 0 else []
            p.t(lambda t, i=i, pb=pb, st=(i == 0): t.matmul(
                av["ps2_mm"][pb][:, :], lhs_list[i](), rhs_list[i](),
                start=(i == 0), stop=False), waits=w)
        lf, rf = bias_pair
        p.t(lambda t, pb=pb: t.matmul(av["ps2_mm"][pb][:, :], lf(), rf(),
                                      start=False, stop=True))
        return p.c("T")

    bankap = [av["ps2_mm"][0], av["ps2_mm"][1]] + [av["ps2_o"][h] for h in range(4)]
    bank_wait = [None] * 6
    unit_mm = [0] * 24

    def front_loads(tnext):
        for bb in range(4):
            n0 = 32 * tnext + 8 * bb
            src_ap = a2a_out.ap()[:, :, n0:n0 + 8, :].transpose([2, 0, 1, 3])
            xload_mark[bb] = p.dma("gpsimd", f"Cxb{bb}", lambda g, src_ap=src_ap, bb=bb:
                g.dma_start(out=av["x1bf"][:, bb, :], in_=src_ap),
                waits=[("CC", M["a2a_done"]), (f"Ct{bb % 2}", ct_full[bb % 2])])

    def front_tr():
        for bb in range(4):
            sl = bb % 2
            for j in range(8):
                p.dma("sync", f"Ct{sl}", lambda y, bb=bb, j=j: y.dma_start(
                    out=av["x1f"][:, j, 128 * bb:128 * (bb + 1)],
                    in_=av["x1bf"][:, bb, 128 * j:128 * (j + 1)], transpose=True),
                    waits=[(f"Cxb{bb}", xload_mark[bb]), ("T", xf_mms[0])])
        ct_full[0] = p.c("Ct0")
        ct_full[1] = p.c("Ct1")
        return [("Ct0", ct_full[0]), ("Ct1", ct_full[1])]

    def bank_init():
        bank_wait[0] = [ev_ps[0]]
        bank_wait[1] = [ev_ps[1]]
        for h in range(4):
            bank_wait[2 + h] = [("Sc", oe_last[h]), ("V", v_ps2o[h])]

    def emit_unit(i, xf_marks):
        j = i % 6
        bank = bankap[j]
        w0 = list(bank_wait[j]) + list(xf_marks) + [("Ccg", mCG)]
        if i < 8 or i >= 16:
            dk = i if i < 8 else i - 8
            ii = widx[0]
            chw = f"Cw{ii % 2}"
            mw = p.dma("gpsimd", chw, lambda g, dk=dk, ii=ii: g.dma_start(
                out=av["wibuf"][:, ii % 2, :, :],
                in_=wiT_sp.ap()[:, :, 128 * dk:128 * (dk + 1)]),
                waits=[("Cwisp0", mWISP0), ("Cwisp1", mWISP1), ("T", wi_free.get(ii - 2, 0))])
            widx[0] += 1
            for cc in range(8):
                p.t(lambda t, cc=cc, ii=ii, bank=bank, st=(cc == 0): t.matmul(
                    bank[:, :], av["wibuf"][:, ii % 2, cc, :], av["x1f"][:, cc, 0:512],
                    start=(cc == 0), stop=False),
                    waits=(w0 + [(chw, mw)]) if cc == 0 else [])
            p.t(lambda t, dk=dk, bank=bank: t.matmul(
                bank[:, :], c2["inb"][0:1, 128 * dk:128 * (dk + 1)], ones[0:1, 0:512],
                start=False, stop=True))
            wi_free[ii] = p.c("T")
            unit_mm[i] = wi_free[ii]
        else:
            chv = (i - 8) // 4
            bb = (i - 8) % 4
            if bb == 0:
                mwv = p.dma("gpsimd", f"Cwv{chv}", lambda g, chv=chv: g.dma_start(
                    out=av["wvbuf"][:, 0, :, :],
                    in_=wiT_sp.ap()[:, :, 2 * C + 512 * chv:2 * C + 512 * (chv + 1)]),
                    waits=[("Cwisp0", mWISP0), ("Cwisp1", mWISP1), ("T", p.c("T"))])
                M[("mwv", chv)] = mwv
            mwv = M[("mwv", chv)]
            for cc in range(8):
                p.t(lambda t, cc=cc, bb=bb, bank=bank: t.matmul(
                    bank[:, :], av["x1f"][:, cc, 128 * bb:128 * (bb + 1)],
                    av["wvbuf"][:, 0, cc, :], start=(cc == 0), stop=False),
                    waits=(w0 + [(f"Cwv{chv}", mwv)]) if cc == 0 else [])
            p.t(lambda t, chv=chv, bank=bank: t.matmul(
                bank[:, :], ones[0:1, 0:128],
                c2["inb"][0:1, 2 * C + 512 * chv:2 * C + 512 * (chv + 1)],
                start=False, stop=True))
            unit_mm[i] = p.c("T")
        bank_wait[j] = None

    def emit_evac(i, extra=()):
        j = i % 6
        bank = bankap[j]
        if i < 8:
            m = p.s(lambda sc, i=i, bank=bank: sc.activation(
                av["q2f"][:, i, :], bank[:, :], AF.Copy),
                waits=[("T", unit_mm[i])] + list(extra))
        elif i < 16:
            chv = (i - 8) // 4
            bb = (i - 8) % 4
            p.s(lambda sc, chv=chv, bb=bb, bank=bank: sc.activation(
                av["v2t"][:, bb, 2 * chv, 0:256], bank[:, 0:256], AF.Copy),
                waits=[("T", unit_mm[i]), ("V", mPRE)] + list(extra))
            m = p.s(lambda sc, chv=chv, bb=bb, bank=bank: sc.activation(
                av["v2t"][:, bb, 2 * chv + 1, 0:256], bank[:, 256:512], AF.Copy))
        else:
            dk = i - 16
            m = p.s(lambda sc, dk=dk, bank=bank: sc.activation(
                av["k2f"][:, dk, :], bank[:, :], AF.Copy),
                waits=[("T", unit_mm[i])] + list(extra))
        bank_wait[j] = [("Sc", m)]
        return m

    x1_cnt2 = dict(M.get("x1_cnt2", {0: 0, 1: 0}))
    xt_prev = [x1_cnt2[0], x1_cnt2[1]]
    xf_marks0 = []
    for bb in range(4):
        n0 = 8 * bb
        sl = bb % 2
        src_ap = a2a_out.ap()[:, :, n0:n0 + 8, :].transpose([2, 0, 1, 3])
        mxl = p.dma("gpsimd", f"Cxb{bb}", lambda g, src_ap=src_ap, bb=bb: g.dma_start(
            out=av["x1bf"][:, bb, :], in_=src_ap),
            waits=[("CC", M["a2a_done"]), ("Cwtt0", mWOT), ("Cwtt1", mWOT1),
                   ("Cwisp0", mWISP0), ("Cwisp1", mWISP1)])
        for j in range(8):
            p.dma("sync", f"Ct{sl}", lambda y, bb=bb, sl=sl, j=j: y.dma_start(
                out=av["x1f"][:, j, 128 * bb:128 * (bb + 1)],
                in_=av["x1bf"][:, bb, 128 * j:128 * (j + 1)], transpose=True),
                waits=[(f"Cxb{bb}", mxl)])
    ct_full[0] = p.c("Ct0")
    ct_full[1] = p.c("Ct1")
    xf_marks0 = [("Ct0", ct_full[0]), ("Ct1", ct_full[1])]
    for dk in range(16):
        ii = widx[0]
        chw = f"Cw{ii % 2}"
        mw = p.dma("gpsimd", chw, lambda g, dk=dk, ii=ii: g.dma_start(
            out=av["wibuf"][:, ii % 2, :, :], in_=wiT_sp.ap()[:, :, 128 * dk:128 * (dk + 1)]),
            waits=[("Cwisp0", mWISP0), ("Cwisp1", mWISP1), ("T", wi_free.get(ii - 2, 0))])
        widx[0] += 1
        pb = dk % 2
        mm = big_mm2(
            pb,
            [(lambda cc=cc, ii=ii: av["wibuf"][:, ii % 2, cc, :]) for cc in range(8)],
            [(lambda cc=cc: av["x1f"][:, cc, 0:512]) for cc in range(8)],
            ((lambda dk=dk: c2["inb"][0:1, 128 * dk:128 * (dk + 1)]),
             (lambda: ones[0:1, 0:512])),
            [(chw, mw), ("Ccg", mCG)] + xf_marks0)
        wi_free[ii] = mm
        dst = av["q2f"] if dk < 8 else av["k2f"]
        ev_ps[pb] = ("V", p.v(lambda v, dst=dst, dk=dk, pb=pb: v.tensor_copy(
            dst[:, dk % 8, :], av["ps2_mm"][pb][:, :]), waits=[("T", mm)]))
    for chv in range(2):
        mwv = p.dma("gpsimd", f"Cwv{chv}", lambda g, chv=chv: g.dma_start(
            out=av["wvbuf"][:, 0, :, :],
            in_=wiT_sp.ap()[:, :, 2 * C + 512 * chv:2 * C + 512 * (chv + 1)]),
            waits=[("Cwisp0", mWISP0), ("Cwisp1", mWISP1), ("T", p.c("T"))])
        for bb in range(4):
            pb = (bb * 2 + chv) % 2
            mm = big_mm2(
                pb,
                [(lambda cc=cc, bb=bb: av["x1f"][:, cc, 128 * bb:128 * (bb + 1)]) for cc in range(8)],
                [(lambda cc=cc: av["wvbuf"][:, 0, cc, :]) for cc in range(8)],
                ((lambda: ones[0:1, 0:128]),
                 (lambda chv=chv: c2["inb"][0:1, 2 * C + 512 * chv:2 * C + 512 * (chv + 1)])),
                [(f"Cwv{chv}", mwv)] + xf_marks0)
            ev_ps[pb] = ("V", p.v(lambda v, bb=bb, chv=chv, pb=pb: v.tensor_copy(
                av["v2t"][:, bb, 2 * chv:2 * chv + 2, 0:256], av["ps2_mm"][pb][:, :]),
                waits=[("T", mm), ("V", mPRE)]))
    xf_mms[0] = p.c("T")
    mQKV_t[0] = ("V", p.c("V"))

    for tau in range(8):
        t0 = 512 * tau
        for bb in range(4):
            for h in range(4):
                sb = (bb * 4 + h) % 2
                for i in range(2):
                    p.t(lambda t, bb=bb, h=h, i=i, sb=sb: t.matmul(
                        av["ps2_s"][sb][:, 0:128], av["k2f"][:, 2 * h + i, 128 * bb:128 * (bb + 1)],
                        av["q2f"][:, 2 * h + i, 128 * bb:128 * (bb + 1)],
                        start=(i == 0), stop=(i == 1)),
                        waits=([("Sc", ss_free[sb]), mQKV_t[0], ("V", carryV[0])] if i == 0 else []))
                mqk = p.c("T")
                me = p.s(lambda sc, sb=sb: sc.activation(
                    av["pT2"][:, sb, :], av["ps2_s"][sb][:, 0:128], AF.Exp,
                    bias=negb[:, :], scale=0.0625),
                    waits=[("T", mqk), ("T", pT_free[sb])])
                ss_free[sb] = me
                mmk = p.v(lambda v, sb=sb: v.tensor_tensor(
                    av["pT2"][:, sb, :], av["pT2"][:, sb, :], bd16, op=OP.mult),
                    waits=[("Sc", me)])
                p.t(lambda t, bb=bb, h=h, sb=sb: t.matmul(
                    av["ps2_o"][h][:, 0:257], av["pT2"][:, sb, :], av["v2t"][:, bb, h, :],
                    start=True, stop=True),
                    waits=[("V", mmk), ("Sc", oe_last[h])])
                pT_free[sb] = p.c("T")
                mrc = p.v(lambda v, h=h: v.reciprocal(
                    av["linv2"][:, h:h + 1], av["ps2_o"][h][:, 256:257]),
                    waits=[("T", pT_free[sb]), ("Sc", linv_rd.get(h, 0))])
                v_ps2o[h] = mrc
                oe_last[h] = p.s(lambda sc, bb=bb, h=h: sc.activation(
                    av["o2tok"][:, bb % 2, 256 * h:256 * (h + 1)], av["ps2_o"][h][:, 0:256],
                    AF.Copy, scale=av["linv2"][:, h:h + 1]),
                    waits=[("V", mrc), (f"Cot{bb % 2}", ot_free[bb % 2])])
                linv_rd[h] = oe_last[h]
            mo = p.c("Sc")
            for j in range(8):
                p.dma("scalar", f"Cot{bb % 2}", lambda y, bb=bb, j=j: y.dma_start(
                    out=av["o2f"][:, j, 128 * bb:128 * (bb + 1)],
                    in_=av["o2tok"][:, bb % 2, 128 * j:128 * (j + 1)], transpose=True),
                    waits=[("Sc", mo), ("T", proj_T[0])])
            ot_free[bb % 2] = p.c(f"Cot{bb % 2}")
        mOT = [("Cot0", p.c("Cot0")), ("Cot1", p.c("Cot1"))]

        if tau < 7:
            front_loads(tau + 1)

        for bb in range(4):
            chx = f"Cxr{bb}"
            mxr = p.dma("gpsimd", chx, lambda g, tau=tau, bb=bb: g.dma_start(
                out=av["xrebf"][:, bb, :],
                in_=a2a_out.ap()[:, :, 32 * tau + 8 * bb:32 * tau + 8 * (bb + 1), :]
                .transpose([2, 0, 1, 3])),
                waits=[("V", xrb_free[bb])])
            for chp in range(2):
                pb = chp
                mm = big_mm2(
                    pb,
                    [(lambda cc=cc, bb=bb: av["o2f"][:, cc, 128 * bb:128 * (bb + 1)]) for cc in range(8)],
                    [(lambda cc=cc, chp=chp: av["woT"][:, cc, 512 * chp:512 * (chp + 1)]) for cc in range(8)],
                    ((lambda: ones[0:1, 0:128]),
                     (lambda chp=chp: c2["outb"][0:1, 512 * chp:512 * (chp + 1)])),
                    list(mOT) + [("Cwtt0", mWOT), ("Cwtt1", mWOT1)])
                mev = p.v(lambda v, bb=bb, chp=chp, pb=pb: v.tensor_tensor(
                    av["xres2"][:, bb, 512 * chp:512 * (chp + 1)], av["ps2_mm"][pb][:, :],
                    av["xrebf"][:, bb, 512 * chp:512 * (chp + 1)], op=OP.add),
                    waits=[("T", mm), (chx, mxr)])
                ev_ps[pb] = ("V", mev)
            xrb_free[bb] = mev
            mres = mev
            p.v(lambda v, bb=bb: (v.tensor_reduce(
                c2["s1b"][:, bb:bb + 1], av["xres2"][:, bb, :], axis=AX, op=OP.add),
                v.drain())[-1], waits=[("V", stat_rd[0])])
            p.s(lambda sc, bb=bb: (sc.activation(
                c2["junk2"][:, :], av["xres2"][:, bb, :], AF.Square,
                accum_out=c2["s2b"][:, bb:bb + 1]), sc.drain())[-1],
                waits=[("V", mres), ("V", stat_rd[0])])
        proj_T[0] = p.c("T")
        mSQ = p.c("Sc")

        if tau < 7:
            xf_marks = front_tr()
            bank_init()
            for i in range(6):
                emit_unit(i, xf_marks)

        ln_chain(p, c2, mSQ)

        if tau < 7:
            for i in range(6, 22):
                emit_evac(i - 6)
                emit_unit(i, xf_marks)

        for bb in range(4):
            blkg = 4 * tau + bb
            mh = p.v(lambda v, bb=bb: v.tensor_scalar(
                av["hfull"][:, :], av["xres2"][:, bb, :],
                c2["meanb"][:, bb:bb + 1], c2["rstdb"][:, bb:bb + 1],
                op0=OP.subtract, op1=OP.mult),
                waits=[(f"Cyh{bb % 2}", yh_free[bb % 2]), (f"Cyl{bb % 2}", yl_free[bb % 2])])
            p.v(lambda v, bb=bb: v.tensor_copy(av["htok"][:, bb % 2, :], av["hfull"][:, :]))
            mlo = p.v(lambda v, bb=bb: v.scalar_tensor_tensor(
                av["lotok"][:, bb % 2, :], av["htok"][:, bb % 2, :], -1.0,
                av["hfull"][:, :], op0=OP.mult, op1=OP.add))
            xr2_free[bb] = mlo
            for j in range(8):
                p.dma("sync", f"Cyh{bb % 2}", lambda y, blkg=blkg, bb=bb, j=j: y.dma_start(
                    out=y1f[:, j, 128 * blkg:128 * (blkg + 1)],
                    in_=av["htok"][:, bb % 2, 128 * j:128 * (j + 1)], transpose=True),
                    waits=[("V", mlo), ("T", gates_T[0])])
                p.dma("sync", f"Cyl{bb % 2}", lambda y, bb=bb, j=j: y.dma_start(
                    out=av["y1lo"][:, j, 128 * bb:128 * (bb + 1)],
                    in_=av["lotok"][:, bb % 2, 128 * j:128 * (j + 1)], transpose=True),
                    waits=[("V", mlo), ("T", gates_T[0])])
            yh_free[bb % 2] = p.c(f"Cyh{bb % 2}")
            yl_free[bb % 2] = p.c(f"Cyl{bb % 2}")
        mYHL = [("Cyh0", p.c("Cyh0")), ("Cyh1", p.c("Cyh1")),
                ("Cyl0", p.c("Cyl0")), ("Cyl1", p.c("Cyl1"))]

        gev = []
        for tkb in range(4):
            blkg = 4 * tau + tkb
            first = True
            for cc in range(8):
                for (lhs, rhs) in ((lambda cc=cc, blkg=blkg: y1f[:, cc, 128 * blkg:128 * (blkg + 1)],
                                    lambda cc=cc: c2["wgh"][:, cc, :]),
                                   (lambda cc=cc, blkg=blkg: y1f[:, cc, 128 * blkg:128 * (blkg + 1)],
                                    lambda cc=cc: c2["wgl"][:, cc, :]),
                                   (lambda cc=cc, tkb=tkb: av["y1lo"][:, cc, 128 * tkb:128 * (tkb + 1)],
                                    lambda cc=cc: c2["wgh"][:, cc, :])):
                    w = (list(mYHL) + [("V", mbwe), ("Sc", ss_free[0]),
                         ("V", gev[-1] if gev else 0)]) if first else []
                    p.t(lambda t, lhs=lhs, rhs=rhs, first=first: t.matmul(
                        av["ps2_s"][0][:, 256:264], lhs(), rhs(),
                        start=first, stop=False), waits=w)
                    first = False
            p.t(lambda t: t.matmul(av["ps2_s"][0][:, 256:264], ones[0:1, 0:128],
                                   c2["bwgb"][0:1, :], start=False, stop=True))
            mg = p.c("T")
            gev.append(p.v(lambda v, tkb=tkb: v.tensor_copy(
                av["gt4"][:, tkb, :], av["ps2_s"][0][:, 256:264]), waits=[("T", mg)]))
        gates_T[0] = p.c("T")

        if tau < 7:
            gw = [("T", gates_T[0])]
            emit_evac(16, gw)
            emit_evac(17, gw)
            emit_unit(22, xf_marks)
            emit_unit(23, xf_marks)
            for i in range(18, 24):
                emit_evac(i, gw)
            xf_mms[0] = p.c("T")
            mQKV_t[0] = ("Sc", p.c("Sc"))

        for cc in range(8):
            p.v(lambda v, cc=cc, t0=t0: v.tensor_scalar(
                y1f[:, cc, t0:t0 + 512], y1f[:, cc, t0:t0 + 512],
                c2["gf"][:, cc:cc + 1], None, op0=OP.mult),
                waits=[("T", gates_T[0])] if cc == 0 else ())
        gsc_V[0] = p.c("V")
        M[("y1f_ready", tau)] = gsc_V[0]

        i9b = av["i9"][:, :].unsqueeze(1).broadcast_to([128, 4, 8])
        iob = c2["iota"][:, :].unsqueeze(1).broadcast_to([128, 4, 8])
        p.v(lambda v: (v.tensor_reduce(av["mx4"][:, :], av["gt4"][:, :, :], axis=AX,
                                       op=OP.max), v.drain())[-1])
        p.v(lambda v: (v.tensor_tensor(
            av["rt1"][:, :, :], av["gt4"][:, :, :],
            av["mx4"][:, :].unsqueeze(2).broadcast_to([128, 4, 8]), op=OP.subtract),
            v.drain())[-1])
        mex = p.s(lambda sc: (sc.activation(av["rt2"][:, :, :], av["rt1"][:, :, :], AF.Exp),
                              sc.drain())[-1], waits=[("V", p.c("V"))])
        p.v(lambda v: (v.tensor_reduce(av["sm4"][:, :], av["rt2"][:, :, :], axis=AX,
                                       op=OP.add), v.drain())[-1], waits=[("Sc", mex)])
        p.v(lambda v, tau=tau: (v.reciprocal(gval[:, 4 * tau:4 * tau + 4], av["sm4"][:, :]),
                                v.drain())[-1])
        p.v(lambda v: (v.tensor_tensor(
            av["rt1"][:, :, :], av["gt4"][:, :, :],
            av["mx4"][:, :].unsqueeze(2).broadcast_to([128, 4, 8]), op=OP.is_ge),
            v.drain())[-1])
        p.v(lambda v, i9b=i9b: (v.tensor_tensor(av["rt2"][:, :, :], av["rt1"][:, :, :],
                                                i9b, op=OP.mult), v.drain())[-1])
        p.v(lambda v: (v.tensor_scalar(av["rt2"][:, :, :], av["rt2"][:, :, :], 9.0, None,
                                       op0=OP.add), v.drain())[-1])
        p.v(lambda v: (v.tensor_reduce(av["mx4"][:, :], av["rt2"][:, :, :], axis=AX,
                                       op=OP.min), v.drain())[-1])
        mohv = p.v(lambda v, iob=iob, tau=tau: (v.tensor_tensor(
            ohb[:, 4 * tau:4 * tau + 4, :], iob,
            av["mx4"][:, :].unsqueeze(2).broadcast_to([128, 4, 8]), op=OP.is_equal),
            v.drain())[-1])
        for bb in range(4):
            blkg = 4 * tau + bb
            p.t(lambda t, blkg=blkg: t.matmul(
                av["ps2_s"][1][:, 0:8], tri_s, ohb[:, blkg, :], start=True, stop=False),
                waits=[("V", mohv), ("V", carryV[0]), ("Sc", ss_free[1])])
            p.t(lambda t: t.matmul(av["ps2_s"][1][:, 0:8], c2["bpat"][:, :], carry[:, :],
                                   start=False, stop=True))
            p.t(lambda t, blkg=blkg: t.matmul(
                av["ps2_s"][1][0:16, 8:16], c2["bpatT"][:, :], ohb[:, blkg, :],
                start=True, stop=True))
            mcnt = p.c("T")
            cntev[blkg] = p.v(lambda v, blkg=blkg: (v.tensor_copy(
                cnts[:, blkg, :], av["ps2_s"][1][:, 0:8]), v.drain())[-1],
                waits=[("T", mcnt)])
            carryV[0] = p.v(lambda v: (v.tensor_tensor(
                carry[:, :], carry[:, :], av["ps2_s"][1][0:16, 8:16], op=OP.add),
                v.drain())[-1])
        stat_rd[0] = p.c("V")

    if DBG_PHASE == 2:
        allw2 = [("V", p.c("V")), ("T", p.c("T")), ("Sc", p.c("Sc"))]
        p.dma("gpsimd", "Cdbg", lambda g: g.dma_start(out=A["dbg_wot"], in_=av["woT"][:, :, :]),
              waits=allw2)
        p.dma("gpsimd", "Cdbg", lambda g: g.dma_start(out=A["dbg_q2f"], in_=av["q2f"][:, :, :]))
        p.dma("gpsimd", "Cdbg", lambda g: g.dma_start(out=A["dbg_gf"], in_=c2["gf"][:, :]))
        p.dma("gpsimd", "Cdbg", lambda g: g.dma_start(out=A["dbg_wgh"], in_=c2["wgh"][:, :, :]))
        p.dma("gpsimd", "Cdbg", lambda g: g.dma_start(out=A["dbg_x1f"], in_=av["x1f"][:, :, :]))
    M["p2a_T"] = p.c("T")
    M["carry_done"] = carryV[0]
    M["cntev"] = dict(cntev)



def ln_chain(p, c2, mSQ):
    p.v(lambda v: (v.tensor_scalar(c2["meanb"][:, :], c2["s1b"][:, :], 1.0 / C, None,
                                   op0=OP.mult), v.drain())[-1])
    p.v(lambda v: (v.tensor_tensor(c2["m2tb"][:, :], c2["meanb"][:, :], c2["meanb"][:, :],
                                   op=OP.mult), v.drain())[-1])
    p.v(lambda v: (v.scalar_tensor_tensor(
        c2["varb"][:, :], c2["s2b"][:, :], 1.0 / C, c2["m2tb"][:, :],
        op0=OP.mult, op1=OP.subtract), v.drain())[-1], waits=[("Sc", mSQ)])
    mv = p.v(lambda v: (v.tensor_scalar(c2["varb"][:, :], c2["varb"][:, :], EPS, None,
                                        op0=OP.add), v.drain())[-1])
    ms = p.s(lambda sc: (sc.activation(c2["rstdb"][:, :], c2["varb"][:, :], AF.Sqrt,
                                       scale=1.0), sc.drain())[-1], waits=[("V", mv)])
    p.v(lambda v: (v.reciprocal(c2["rstdb"][:, :], c2["rstdb"][:, :]), v.drain())[-1],
        waits=[("Sc", ms)])
    p.v(lambda v: (v.tensor_tensor(c2["m2tb"][:, :], c2["rstdb"][:, :], c2["rstdb"][:, :],
                                   op=OP.mult), v.drain())[-1])
    p.v(lambda v: (v.tensor_tensor(c2["m2tb"][:, :], c2["m2tb"][:, :], c2["varb"][:, :],
                                   op=OP.mult), v.drain())[-1])
    p.v(lambda v: (v.tensor_scalar(c2["m2tb"][:, :], c2["m2tb"][:, :], -0.5, 1.5,
                                   op0=OP.mult, op1=OP.add), v.drain())[-1])
    return p.v(lambda v: (v.tensor_tensor(c2["rstdb"][:, :], c2["rstdb"][:, :],
                                          c2["m2tb"][:, :], op=OP.mult), v.drain())[-1])


def gen_route_finish(p, A, cst, c2, bv2, drs, M):
    p.v(lambda v: v.memset(c2["coefs"][:, :], 1.0))
    cnts, gval, ohb = c2["cnts"], c2["gval"], c2["ohb"]
    coefm, coefs, carry, offsel = c2["coefm"], c2["coefs"], c2["carry"], c2["offsel"]
    CAP5 = CAPACITY + 0.5
    mcar = p.dma("gpsimd", "Cct", lambda g: g.dma_start(
        out=drs["cnt_in"].ap(), in_=carry[:, :]), waits=[("V", M["carry_done"])])
    p.ev("gpsimd", lambda g: g.collective_compute(
        "AllGather", OP.bypass, replica_groups=[list(range(NCORES))],
        ins=[drs["cnt_in"].ap().opt()], outs=[drs["cnt_ag"].ap().opt()]),
        waits=[("Cct", mcar)], inc=("CC", 1))
    magc = p.c("CC")
    mcw = p.dma("gpsimd", "Cct", lambda g: g.dma_start(
        out=c2["cw"][:, :, :], in_=drs["cnt_ag"].ap().transpose([1, 2, 0]),
        allow_slow_non_contiguous=True),
        waits=[("CC", magc)])
    p.v(lambda v: (v.tensor_tensor(c2["cw2"][:, :, :], c2["cw"][:, :, :],
                                   c2["rke"][:, :, :], op=OP.mult), v.drain())[-1],
        waits=[("Cct", mcw)])
    p.v(lambda v: (v.tensor_reduce(c2["rk16"][:, :], c2["cw2"][:, :, :], axis=AX,
                                   op=OP.add), v.drain())[-1])
    mtot = p.v(lambda v: (v.tensor_reduce(c2["tot16"][:, :], c2["cw"][:, :, :], axis=AX,
                                          op=OP.add), v.drain())[-1])
    p.t(lambda t: t.matmul(bv2["zps"][0][0:16, 0:8], c2["btri"][:, :], c2["tot16"][:, :],
                           start=True, stop=True), waits=[("V", mtot)])
    mob = p.c("T")
    moff = p.v(lambda v: (v.tensor_tensor(c2["off16"][:, :], bv2["zps"][0][0:16, 0:8],
                                          c2["rk16"][:, :], op=OP.add), v.drain())[-1],
               waits=[("T", mob)])
    p.t(lambda t: t.matmul(bv2["zps"][1][:, 0:8], c2["bpat"][:, :], c2["off16"][:, :],
                           start=True, stop=True), waits=[("V", moff)])
    mos = p.c("T")
    mosel = p.v(lambda v: (v.tensor_copy(offsel[:, :], bv2["zps"][1][:, 0:8]),
                           v.drain())[-1], waits=[("T", mos)])
    trev = [0]
    for blkg in range(32):
        p.v(lambda v, blkg=blkg: (v.tensor_tensor(
            bv2["rtb"][:, :], cnts[:, blkg, :], offsel[:, :], op=OP.add), v.drain())[-1],
            waits=[("V", mosel), ("V", M["cntev"][blkg])])
        p.v(lambda v: (v.tensor_scalar(bv2["rtb"][:, :], bv2["rtb"][:, :], CAP5, None,
                                       op0=OP.is_lt), v.drain())[-1])
        p.v(lambda v, blkg=blkg: (v.tensor_tensor(
            bv2["rtb"][:, :], bv2["rtb"][:, :], ohb[:, blkg, :], op=OP.mult), v.drain())[-1])
        mcoef = p.v(lambda v, blkg=blkg: (v.tensor_scalar(
            coefm[:, blkg, :], bv2["rtb"][:, :], gval[:, blkg:blkg + 1], None,
            op0=OP.mult), v.drain())[-1])
        p.t(lambda t, blkg=blkg: t.transpose(
            bv2["zps"][2][0:8, 0:128], coefm[:, blkg, :], c2["idf"][:, :]),
            waits=[("V", mcoef), ("V", trev[0])])
        mtr = p.c("T")
        trev[0] = p.v(lambda v, blkg=blkg: (v.tensor_copy(
            coefs[0:8, 128 * blkg:128 * (blkg + 1)], bv2["zps"][2][0:8, 0:128]),
            v.drain())[-1], waits=[("T", mtr)])
    M["coef_done"] = p.c("V")
    if DBG_PHASE == 2:
        allw = [("V", p.c("V")), ("T", p.c("T")), ("Sc", p.c("Sc"))]
        p.dma("gpsimd", "Cdbg", lambda g: g.dma_start(
            out=A["dbg_a2o"], in_=drs["a2a_out"].ap()), waits=[("CC", M["a2a_done"])])
        p.dma("gpsimd", "Cdbg", lambda g: g.dma_start(
            out=A["dbg_wit"], in_=drs["wiT_sp"].ap()))
        for nm, tile in [("dbg_y1f", c2["y1f"]), ("dbg_cnts", cnts), ("dbg_gval", gval),
                         ("dbg_ohb", ohb), ("dbg_coefm", coefm), ("dbg_offsel", offsel)]:
            p.dma("gpsimd", "Cdbg", lambda g, nm=nm, tile=tile: g.dma_start(
                out=A[nm], in_=tile[:, :] if nm in ("dbg_gval", "dbg_offsel") else tile[:, :, :]),
                waits=allw)




def gen_phase2bc(p, A, cst, c2, bv2, drs, M):
    ones = cst["ones"]
    y1f, coefm, coefs = c2["y1f"], c2["coefm"], c2["coefs"]
    wc, ztile, y1tok, h2 = bv2["wc"], bv2["ztile"], bv2["y1tok"], bv2["h2"]
    x1rl, outtok, zps = bv2["x1rl"], bv2["outtok"], bv2["zps"]
    a2a_out, wcs, out = drs["a2a_out"], drs["wcs"], drs["out"]

    zT = {}
    zev = [0, 0, 0, 0]
    ytb_free = [0, 0]
    stat_rd = [0]
    x1r_free = [0, 0]
    out_cnt = [0, 0]
    widx = [0]
    ztile_free = [0, 0]
    h2_free = [0]
    zsl = 0

    for tau in range(8):
        zs = (tau % 2) * 4
        for e in range(E):
            i = widx[0]
            chz = f"Cze{i % 2}"
            mw = p.dma("gpsimd", chz, lambda g, e=e, i=i: g.dma_start(
                out=wc[:, i % 2, :, :], in_=wcs.ap()[e].rearrange("(k p) d -> p k d", p=128)),
                waits=[("CC", M["ag_done"]), ("T", zT.get(i - 2, 0))])
            widx[0] += 1
            for tb in range(4):
                blkg = 4 * tau + tb
                for chp in range(2):
                    pz = (tb % 2) * 2 + chp
                    for cc in range(8):
                        w = []
                        if cc == 0:
                            w = [(chz, mw), ("V", M[("y1f_ready", tau)]),
                                 ("V", M["coef_done"]), ("V", zev[pz])]
                            if e == 0:
                                w.append(("V", ztile_free[tau % 2]))
                        p.t(lambda t, cc=cc, blkg=blkg, chp=chp, i=i, pz=pz: t.matmul(
                            zps[pz][:, :], y1f[:, cc, 128 * blkg:128 * (blkg + 1)],
                            wc[:, i % 2, cc, 512 * chp:512 * (chp + 1)],
                            start=(cc == 0), stop=(cc == 7)), waits=w)
                    mz = p.c("T")
                    if e == 0:
                        zev[pz] = p.v(lambda v, tb=tb, chp=chp, blkg=blkg, e=e, pz=pz, zs=zs: v.tensor_scalar(
                            ztile[:, zs + tb, 512 * chp:512 * (chp + 1)], zps[pz][:, :],
                            coefm[:, blkg, e:e + 1], None, op0=OP.mult),
                            waits=[("T", mz)])
                    else:
                        zev[pz] = p.v(lambda v, tb=tb, chp=chp, blkg=blkg, e=e, pz=pz, zs=zs: v.scalar_tensor_tensor(
                            ztile[:, zs + tb, 512 * chp:512 * (chp + 1)], zps[pz][:, :],
                            coefm[:, blkg, e:e + 1],
                            ztile[:, zs + tb, 512 * chp:512 * (chp + 1)],
                            op0=OP.mult, op1=OP.add), waits=[("T", mz)])
            zT[i] = p.c("T")
        for tb in range(4):
            blkg = 4 * tau + tb
            for chp in range(2):
                pz = (tb % 2) * 2 + chp
                p.t(lambda t, blkg=blkg, chp=chp, pz=pz: t.matmul(
                    zps[pz][:, :], coefs[0:9, 128 * blkg:128 * (blkg + 1)],
                    c2["biasm"][0:9, 512 * chp:512 * (chp + 1)], start=True, stop=True),
                    waits=[("V", zev[pz]), ("V", M["coef_done"]), ("Ccg", M["mCG"])])
                mz = p.c("T")
                zev[pz] = p.v(lambda v, tb=tb, chp=chp, pz=pz, zs=zs: v.tensor_tensor(
                    ztile[:, zs + tb, 512 * chp:512 * (chp + 1)], zps[pz][:, :],
                    ztile[:, zs + tb, 512 * chp:512 * (chp + 1)], op=OP.add),
                    waits=[("T", mz)])
            chy = f"Cyh{tb % 2}"
            for j in range(8):
                p.dma("sync", chy, lambda y, blkg=blkg, tb=tb, j=j: y.dma_start(
                    out=y1tok[:, tb % 2, 128 * j:128 * (j + 1)],
                    in_=y1f[:, j, 128 * blkg:128 * (blkg + 1)], transpose=True),
                    waits=[("V", ytb_free[tb % 2])])
            mytb = p.c(chy)
            mr3 = p.v(lambda v, tb=tb, zs=zs: v.tensor_tensor(
                ztile[:, zs + tb, :], y1tok[:, tb % 2, :], ztile[:, zs + tb, :], op=OP.add),
                waits=[(chy, mytb)])
            ytb_free[tb % 2] = mr3
            p.v(lambda v, tb=tb, zs=zs: (v.tensor_reduce(
                c2["s1b"][:, tb:tb + 1], ztile[:, zs + tb, :], axis=AX, op=OP.add),
                v.drain())[-1], waits=[("V", stat_rd[0])])
            p.s(lambda sc, tb=tb, zs=zs: (sc.activation(
                c2["junk2"][:, :], ztile[:, zs + tb, :], AF.Square,
                accum_out=c2["s2b"][:, tb:tb + 1]), sc.drain())[-1],
                waits=[("V", mr3), ("V", stat_rd[0])])
        mSQ = p.c("Sc")
        ln_chain(p, c2, mSQ)
        for tb in range(4):
            mh = p.v(lambda v, tb=tb, zs=zs: v.tensor_scalar(
                h2[:, tb, :], ztile[:, zs + tb, :], c2["meanb"][:, tb:tb + 1],
                c2["rstdb"][:, tb:tb + 1], op0=OP.subtract, op1=OP.mult),
                waits=[("V", h2_free[0])])
            mg1 = p.g(lambda g, tb=tb: g.tensor_tensor(
                h2[:, tb, :], h2[:, tb, :], c2["g2b"][:, :], op=OP.mult),
                waits=[("V", mh)])
            mg2 = p.g(lambda g, tb=tb: g.tensor_tensor(
                h2[:, tb, :], h2[:, tb, :], c2["b2b"][:, :], op=OP.add))
            chr_ = f"Cx1r{tb % 2}"
            mxr = p.dma("gpsimd", chr_, lambda g, tau=tau, tb=tb: g.dma_start(
                out=x1rl[:, tb % 2, :],
                in_=a2a_out.ap()[:, :, 32 * tau + 8 * tb:32 * tau + 8 * (tb + 1), :]
                .transpose([2, 0, 1, 3])),
                waits=[("V", x1r_free[tb % 2])])
            mr4 = p.v(lambda v, tb=tb: v.tensor_tensor(
                h2[:, tb, :], x1rl[:, tb % 2, :], h2[:, tb, :], op=OP.add),
                waits=[("G", mg2), (chr_, mxr)])
            x1r_free[tb % 2] = mr4
            p.v(lambda v, tb=tb: (v.tensor_reduce(
                c2["s1b"][:, tb:tb + 1], h2[:, tb, :], axis=AX, op=OP.add),
                v.drain())[-1])
            p.s(lambda sc, tb=tb: (sc.activation(
                c2["junk2"][:, :], h2[:, tb, :], AF.Square,
                accum_out=c2["s2b"][:, tb:tb + 1]), sc.drain())[-1],
                waits=[("V", mr4)])
        ztile_free[tau % 2] = p.c("V")
        mSQ = p.c("Sc")
        stat_rd[0] = ln_chain(p, c2, mSQ)
        for tb in range(4):
            cho = f"Cout{tb % 2}"
            mo = p.v(lambda v, tb=tb: v.tensor_scalar(
                outtok[:, tb % 2, :], h2[:, tb, :], c2["meanb"][:, tb:tb + 1],
                c2["rstdb"][:, tb:tb + 1], op0=OP.subtract, op1=OP.mult),
                waits=[(cho, 16 * out_cnt[tb % 2])])
            n0 = 32 * tau + 8 * tb
            p.dma("scalar", cho, lambda sc, n0=n0, tb=tb: sc.dma_start(
                out=out[:, n0:n0 + 8, :].transpose([1, 0, 2]),
                in_=outtok[:, tb % 2, :]), waits=[("V", mo)])
            out_cnt[tb % 2] += 1
        h2_free[0] = p.c("V")
    p.ev("gpsimd", None, waits=[("Cout0", 16 * out_cnt[0]), ("Cout1", 16 * out_cnt[1])])
    p.ev("scalar", None, waits=[("Cout0", 16 * out_cnt[0]), ("Cout1", 16 * out_cnt[1])])

_BUILT = None


def kernel(**inputs):
    global _BUILT
    core_inputs = []
    x = np.ascontiguousarray(inputs["L0_attn_x"])
    for c in range(NCORES):
        d = host_constants(c)
        d.update(dict(
            x=x[2 * c:2 * c + 2],
            w_attn=inputs["L0_attn_w_attn"], b_attn=inputs["L0_attn_b_attn"],
            w_proj=inputs["L0_attn_w_proj"], b_proj=inputs["L0_attn_b_proj"],
            in_w=inputs["L0_ffn_in_proj_w"], in_b=inputs["L0_ffn_in_proj_b"],
            out_w=inputs["L0_ffn_out_proj_w"], out_b=inputs["L0_ffn_out_proj_b"],
            ln1_g=inputs["L0_ffn_ln1_g"], ln1_b=inputs["L0_ffn_ln1_b"],
            ln2_g=inputs["L0_ffn_ln2_g"], ln2_b=inputs["L0_ffn_ln2_b"],
            wg=inputs["L0_ffn_wg"],
            w1_own=inputs["L0_ffn_w1"][c], b1_own=inputs["L0_ffn_b1"][c],
            w2_own=inputs["L0_ffn_w2"][c], b2_own=inputs["L0_ffn_b2"][c],
        ))
        core_inputs.append({k: np.ascontiguousarray(v) for k, v in d.items()})

    with allow_ici_all_to_all():
        if _BUILT is None:
            _BUILT = build_nc()
        res = run_bass_kernel_spmd(_BUILT, core_inputs, core_ids=list(range(NCORES)))

    if DBG_PHASE == 1:
        return [res.results[c]["dbg_x1"] for c in range(NCORES)]
    if DBG_PHASE == 2:
        return res
    outp = np.empty((B, T, C), dtype=np.float32)
    for c in range(NCORES):
        outp[:, NSL * c:NSL * (c + 1), :] = res.results[c]["out"]
    return outp


# version 13
# speedup vs baseline: 1.0953x; 1.0953x over previous
import contextlib

import numpy as np
import ml_dtypes

import concourse.bass as bass
import concourse.mybir as mybir
from concourse.alu_op_type import AluOpType
from concourse.bass_utils import run_bass_kernel_spmd
from concourse.aot_env import allow_ici_all_to_all

F32 = mybir.dt.float32
BF16 = mybir.dt.bfloat16
AF = mybir.ActivationFunctionType
AX = mybir.AxisListType.X
OP = AluOpType

B, T, C = 16, 2048, 1024
NH, HD = 16, 64
H2, HD2 = 4, 256
E, FF, CAPACITY = 8, 4096, 4096
NCORES = 8
SEQ = 2
NSL = 256
NTOK2 = B * NSL
EPS = 1e-5
NEGB = -3.0

DBG_PHASE = 0

ENGS = ("sync", "gpsimd", "tensor", "vector", "scalar")
SEM_NAMES = (
    ["T", "V", "Sc", "G", "CC", "Cdbg"]
    + ["Cc", "Ccm", "Cw1p0", "Cw1p1", "Cw1T", "Cw1U", "Cb1s", "Cwc", "Cwp"]
    + [f"Cx{i}" for i in range(4)] + [f"Ct{i}" for i in range(4)]
    + [f"Cw{i}" for i in range(4)] + ["Cwv0", "Cwv1"] + [f"Cxr{i}" for i in range(4)]
    + ["Cx1s0", "Cx1s1", "Cof"]
    + ["Cwsp0", "Cwsp1", "Cwtt0", "Cwtt1", "Cwisp0", "Cwisp1", "Cct", "Cze0", "Cze1",
       "Cxb0", "Cxb1", "Cxb2", "Cxb3", "Cgt0", "Cgt1", "Cyh0", "Cyh1", "Cyl0", "Cyl1", "Cot0", "Cot1",
       "Cx1r0", "Cx1r1", "Cout0", "Cout1", "Ccg"]
)


class Plan:
    def __init__(self, counts=None):
        self.events = []
        self.cnt = dict(counts or {})

    def c(self, sem):
        return self.cnt.get(sem, 0)

    def ev(self, engine, fn=None, waits=(), inc=None):
        waits = tuple((s, v) for (s, v) in waits if v > 0)
        if inc is not None:
            self.cnt[inc[0]] = self.c(inc[0]) + inc[1]
        self.events.append((engine, fn, waits, inc))
        return self.c(inc[0]) if inc is not None else None

    def t(self, fn, waits=()):
        return self.ev("tensor", fn, waits, ("T", 1))

    def v(self, fn, waits=()):
        return self.ev("vector", fn, waits, ("V", 1))

    def s(self, fn, waits=()):
        return self.ev("scalar", fn, waits, ("Sc", 1))

    def g(self, fn, waits=()):
        return self.ev("gpsimd", fn, waits, ("G", 1))

    def dma(self, engine, ch, fn, waits=()):
        return self.ev(engine, fn, waits, (ch, 16))


def emit(nc, blk, plan, sems, watermarks):
    for name in ENGS:
        deco = getattr(blk, name)

        def make(engine_name):
            def body(eng):
                wm = watermarks.setdefault(engine_name, {})
                for e_name, fn, waits, inc in plan.events:
                    if e_name != engine_name:
                        continue
                    for s, v in waits:
                        if wm.get(s, 0) < v:
                            eng.wait_ge(sems[s], v)
                            wm[s] = v
                    instr = fn(eng) if fn is not None else None
                    if inc is not None:
                        if instr is None:
                            instr = eng.nop()
                        instr.then_inc(sems[inc[0]], inc[1])
            return body

        deco(make(name))


def host_constants(core):
    r = np.arange(128)
    c = np.arange(128)
    tri_causal = (r[:, None] <= c[None, :]).astype(ml_dtypes.bfloat16)
    bp, npp = r % 16, r // 16
    bq, nq = c % 16, c // 16
    tri_s = ((bp[:, None] == bq[None, :]) & (npp[:, None] <= nq[None, :])
             ).astype(ml_dtypes.bfloat16)
    bd16 = ((r[:, None] // 16) == (c[None, :] // 16)).astype(ml_dtypes.bfloat16)
    cmask = np.stack([tri_causal, tri_s, bd16])
    bpat = (np.arange(16)[:, None] == (np.arange(128) % 16)[None, :]).astype(np.float32)
    bpatT = np.ascontiguousarray(bpat.T).astype(ml_dtypes.bfloat16)
    btri = (np.arange(16)[:, None] < np.arange(16)[None, :]).astype(np.float32)
    identf = np.eye(128, dtype=np.float32)
    onesbf = np.ones((1, 512), dtype=ml_dtypes.bfloat16)
    rankexcl = np.zeros((16, 8, 8), dtype=np.float32)
    rankexcl[:, :, :core] = 1.0
    iota8 = np.broadcast_to(np.arange(8, dtype=np.float32), (128, 8)).copy()
    return dict(cmask=cmask, bpatf=bpat, bpatT=bpatT, btri16=btri, identf=identf,
                onesbf=onesbf, rankexcl=rankexcl, iota8=iota8)


def build_nc():
    nc = bass.Bass("TRN2", target_bir_lowering=False)

    A = {}
    def din(name, shape, dt=F32):
        A[name] = nc.dram_tensor(name, shape, dt, kind="ExternalInput").ap()
    din("x", [SEQ, T, C]); din("w_attn", [C, 3 * C]); din("b_attn", [3 * C])
    din("w_proj", [C, C]); din("b_proj", [C])
    din("in_w", [3 * C, C]); din("in_b", [3 * C]); din("out_w", [C, C]); din("out_b", [C])
    din("ln1_g", [C]); din("ln1_b", [C]); din("ln2_g", [C]); din("ln2_b", [C])
    din("wg", [C, E]); din("w1_own", [C, FF]); din("b1_own", [FF])
    din("w2_own", [FF, C]); din("b2_own", [C])
    din("cmask", [3, 128, 128], BF16); din("bpatf", [16, 128]); din("bpatT", [128, 16], BF16)
    din("btri16", [16, 16])
    din("identf", [128, 128]); din("onesbf", [1, 512], BF16)
    din("rankexcl", [16, 8, 8]); din("iota8", [128, 8])

    out_ap = nc.dram_tensor("out", [B, NSL, C], F32, kind="ExternalOutput").ap()
    if DBG_PHASE == 1:
        A["dbg_x1"] = nc.dram_tensor("dbg_x1", [8, SEQ, NSL, C], BF16, kind="ExternalOutput").ap()
    if DBG_PHASE == 2:
        A["dbg_y1f"] = nc.dram_tensor("dbg_y1f", [128, 8, NTOK2], BF16, kind="ExternalOutput").ap()
        A["dbg_cnts"] = nc.dram_tensor("dbg_cnts", [128, 32, 8], F32, kind="ExternalOutput").ap()
        A["dbg_gval"] = nc.dram_tensor("dbg_gval", [128, 32], F32, kind="ExternalOutput").ap()
        A["dbg_ohb"] = nc.dram_tensor("dbg_ohb", [128, 32, 8], BF16, kind="ExternalOutput").ap()
        A["dbg_coefm"] = nc.dram_tensor("dbg_coefm", [128, 32, 8], F32, kind="ExternalOutput").ap()
        A["dbg_offsel"] = nc.dram_tensor("dbg_offsel", [128, 8], F32, kind="ExternalOutput").ap()
        A["dbg_a2o"] = nc.dram_tensor("dbg_a2o", [8, SEQ, NSL, C], F32, kind="ExternalOutput").ap()
        A["dbg_wit"] = nc.dram_tensor("dbg_wit", [128, 8, 3 * C], BF16, kind="ExternalOutput").ap()
        A["dbg_wot"] = nc.dram_tensor("dbg_wot", [128, 8, C], BF16, kind="ExternalOutput").ap()
        A["dbg_q2f"] = nc.dram_tensor("dbg_q2f", [128, 8, 512], BF16, kind="ExternalOutput").ap()
        A["dbg_x1f"] = nc.dram_tensor("dbg_x1f", [128, 8, 512], BF16, kind="ExternalOutput").ap()
        A["dbg_gf"] = nc.dram_tensor("dbg_gf", [128, 16], F32, kind="ExternalOutput").ap()
        A["dbg_wgh"] = nc.dram_tensor("dbg_wgh", [128, 8, 8], BF16, kind="ExternalOutput").ap()

    wc_in = nc.dram_tensor("wc_in", [C, C], BF16)
    wcs = nc.dram_tensor("wcs", [E, C, C], BF16)
    bias_me = nc.dram_tensor("bias_me", [C], F32)
    bias_ag = nc.dram_tensor("bias_ag", [E, C], F32)
    a2a_in = nc.dram_tensor("a2a_in", [8, SEQ, NSL, C], BF16)
    a2a_out = nc.dram_tensor("a2a_out", [8, SEQ, NSL, C], BF16)
    cnt_in = nc.dram_tensor("cnt_in", [16, 8], F32)
    cnt_ag = nc.dram_tensor("cnt_ag", [8, 16, 8], F32)
    wiT_sp = nc.dram_tensor("wiT_sp", [128, 8, 3 * C], BF16)
    drs = dict(wc_in=wc_in, wcs=wcs, bias_me=bias_me, bias_ag=bias_ag,
               a2a_in=a2a_in, a2a_out=a2a_out, cnt_in=cnt_in, cnt_ag=cnt_ag,
               wiT_sp=wiT_sp, out=out_ap)

    M = {}

    with contextlib.ExitStack() as top:
        sems = {s: top.enter_context(nc.semaphore(s)) for s in SEM_NAMES}
        blk = top.enter_context(nc.Block())
        wmarks = {}

        with contextlib.ExitStack() as S0:
            t_ = lambda *a: S0.enter_context(nc.sbuf_tensor(*a))
            cst = {}
            cst["msk"] = t_("msk", [128, 3, 128], BF16)
            cst["ones"] = t_("ones", [1, 512], BF16)
            cst["negb"] = t_("negb", [128, 1], F32)
            cst["eps"] = t_("eps", [128, 1], F32)

            with contextlib.ExitStack() as SC:
                tc = lambda *a: SC.enter_context(nc.sbuf_tensor(*a))
                cv = {}
                cv["w1p"] = tc("w1p", [128, 2, FF], BF16)
                cv["w1T"] = tc("w1T", [128, 32, C], BF16)
                cv["w2s"] = tc("w2s", [128, 32, C], BF16)
                cv["b1s"] = tc("b1s", [128, 32], BF16)
                cv["b1tmp"] = tc("b1tmp", [32, 128], BF16)
                cv["b2s"] = tc("b2s", [1, C], F32)
                cv["wcev"] = tc("wcev", [128, 2, 512], BF16)
                cv["bev"] = tc("bev", [1, 2, 512], F32)
                cv["ps_c"] = [SC.enter_context(nc.psum_tensor(f"ps_c{i}", [128, 512], F32)) for i in range(2)]
                cv["ps_cb"] = SC.enter_context(nc.psum_tensor("ps_cb", [1, 512], F32))

                p = Plan()
                gen_consts(p, A, cst, M)
                gen_combine(p, A, cst, cv, drs, M)
                emit(nc, blk, p, sems, wmarks)
                counts = p.cnt

            with contextlib.ExitStack() as S1:
                t1 = lambda *a: S1.enter_context(nc.sbuf_tensor(*a))
                bv = {}
                bv["xbf"] = t1("xbf", [128, 4, C], BF16)
                bv["x_f"] = t1("x_f", [128, 8, T], BF16)
                bv["k_f"] = t1("k_f", [128, 8, T], BF16)
                bv["v_t"] = t1("v_t", [128, 16, 16, 65], BF16)
                bv["q_fb"] = t1("q_fb", [128, 8, 512], BF16)
                bv["wbuf"] = t1("wbuf", [128, 4, 8, 128], BF16)
                bv["wv"] = t1("wv", [128, 2, 8, 512], BF16)
                bv["wp"] = t1("wp", [128, 8, C], BF16)
                bv["pT"] = t1("pT", [128, 2, 512], BF16)
                bv["o_tok"] = t1("o_tok", [128, 4, 16, 64], BF16)
                bv["o_f"] = t1("o_f", [128, 8, 512], BF16)
                bv["linv"] = t1("linv", [128, 16, 4], F32)
                bv["xres"] = t1("xres", [128, 4, C], F32)
                bv["x1b"] = t1("x1b", [128, 2, C], BF16)
                bv["s1"] = t1("s1", [128, 4], F32)
                bv["s2"] = t1("s2", [128, 4], F32)
                bv["mean"] = t1("mean", [128, 4], F32)
                bv["m2t"] = t1("m2t", [128, 4], F32)
                bv["var"] = t1("var", [128, 4], F32)
                bv["junk"] = t1("junk", [128, C], BF16)
                bv["rstd"] = t1("rstd", [128, 4], F32)
                bv["batt"] = t1("batt", [1, 3 * C], BF16)
                bv["bproj"] = t1("bprojs", [1, C], BF16)
                bv["ps_mm"] = [S1.enter_context(nc.psum_tensor(f"ps_mm{i}", [128, 512], F32)) for i in range(2)]
                bv["ps_s"] = [S1.enter_context(nc.psum_tensor(f"ps_s{i}", [128, 512], F32)) for i in range(2)]
                bv["ps_o"] = [S1.enter_context(nc.psum_tensor(f"ps_o{i}", [128, 512], F32)) for i in range(4)]

                p = Plan(counts)
                gen_phase1(p, A, cst, bv, drs, M)
                if DBG_PHASE == 1:
                    allw = ([("Cx1s0", p.c("Cx1s0")), ("Cx1s1", p.c("Cx1s1")),
                             ("T", p.c("T")), ("V", p.c("V")), ("Sc", p.c("Sc"))])
                    p.dma("gpsimd", "Cdbg", lambda g: g.dma_start(
                        out=A["dbg_x1"], in_=drs["a2a_in"].ap()), waits=allw)
                    p.ev("gpsimd", None, waits=[("Cdbg", p.c("Cdbg"))])
                else:
                    p.ev("gpsimd", lambda g: g.collective_compute(
                        "AllToAll", OP.bypass, replica_groups=[list(range(NCORES))],
                        ins=[drs["a2a_in"].ap().opt()], outs=[drs["a2a_out"].ap().opt()]),
                        waits=[("Cx1s0", p.c("Cx1s0")), ("Cx1s1", p.c("Cx1s1"))],
                        inc=("CC", 1))
                M["a2a_done"] = p.c("CC")
                emit(nc, blk, p, sems, wmarks)
                counts = p.cnt

            with contextlib.ExitStack() as S2o:
                t2o = lambda *a: S2o.enter_context(nc.sbuf_tensor(*a))
                c2 = {}
                c2["y1f"] = t2o("y1f", [128, 8, NTOK2], BF16)
                c2["cnts"] = t2o("cnts", [128, 32, 8], F32)
                c2["gval"] = t2o("gval", [128, 32], F32)
                c2["ohb"] = t2o("ohb", [128, 32, 8], BF16)
                c2["coefm"] = t2o("coefm", [128, 32, 8], F32)
                c2["carry"] = t2o("carry", [16, 8], F32)
                c2["offsel"] = t2o("offsel", [128, 8], F32)
                c2["cw"] = t2o("cw", [16, 8, 8], F32)
                c2["cw2"] = t2o("cw2", [16, 8, 8], F32)
                c2["rk16"] = t2o("rk16", [16, 8], F32)
                c2["tot16"] = t2o("tot16", [16, 8], F32)
                c2["off16"] = t2o("off16", [16, 8], F32)
                c2["bpat"] = t2o("bpat2", [16, 128], F32)
                c2["bpatT"] = t2o("bpatT2", [128, 16], BF16)
                c2["btri"] = t2o("btri2", [16, 16], F32)
                c2["idf"] = t2o("idf2", [128, 128], F32)
                c2["rke"] = t2o("rke2", [16, 8, 8], F32)
                c2["iota"] = t2o("iota2", [128, 8], F32)
                c2["gf"] = t2o("gf2", [128, 16], F32)
                c2["bfb"] = t2o("bfb2", [128, 16], BF16)
                c2["gtmp"] = t2o("gtmp", [16, 128], F32)
                c2["gtmpb"] = t2o("gtmpb", [16, 128], BF16)
                c2["g2b"] = t2o("g2b2", [128, C], BF16)
                c2["b2b"] = t2o("b2b2", [128, C], BF16)
                c2["wgf"] = t2o("wgf2", [128, 8, E], F32)
                c2["wgp"] = t2o("wgp2", [128, 8, E], BF16)
                c2["wgh"] = t2o("wgh2", [128, 8, E], BF16)
                c2["wgl"] = t2o("wgl2", [128, 8, E], BF16)
                c2["bwgb"] = t2o("bwgb", [1, E], BF16)
                c2["biasm"] = t2o("biasm2", [9, C], BF16)
                c2["inb"] = t2o("inb2", [1, 3 * C], BF16)
                c2["outb"] = t2o("outb2", [1, C], BF16)
                c2["s1b"] = t2o("s1b", [128, 4], F32)
                c2["s2b"] = t2o("s2b", [128, 4], F32)
                c2["meanb"] = t2o("meanb", [128, 4], F32)
                c2["m2tb"] = t2o("m2tb", [128, 4], F32)
                c2["varb"] = t2o("varb", [128, 4], F32)
                c2["rstdb"] = t2o("rstdb", [128, 4], F32)
                c2["junk2"] = t2o("junk2", [128, C], BF16)

                with contextlib.ExitStack() as S2a:
                    t2a = lambda *a: S2a.enter_context(nc.sbuf_tensor(*a))
                    av = {}
                    av["woT"] = t2a("woT", [128, 8, C], BF16)
                    av["ps2_mm"] = [S2a.enter_context(nc.psum_tensor(f"ps2mm{i}", [128, 512], F32)) for i in range(2)]
                    av["ps2_s"] = [S2a.enter_context(nc.psum_tensor(f"ps2s{i}", [128, 512], F32)) for i in range(2)]
                    av["ps2_o"] = [S2a.enter_context(nc.psum_tensor(f"ps2o{i}", [128, 512], F32)) for i in range(4)]
                    with contextlib.ExitStack() as S2s:
                        av["wtmp"] = S2s.enter_context(nc.sbuf_tensor("wtmp", [128, 2, C], BF16))
                        av["wiTt"] = S2s.enter_context(nc.sbuf_tensor("wiTt", [128, 2, 8, 128], BF16))
                        p = Plan(counts)
                        gen_phase2a_prep(p, A, cst, c2, av, drs, M)
                        emit(nc, blk, p, sems, wmarks)
                        counts = p.cnt
                    av["wibuf"] = t2a("wibuf", [128, 2, 8, 128], BF16)
                    av["wvbuf"] = t2a("wvbuf", [128, 1, 8, 512], BF16)
                    av["x1bf"] = t2a("x1bf", [128, 4, C], BF16)
                    av["x1f"] = t2a("x1f", [128, 8, 512], BF16)
                    av["q2f"] = t2a("q2f", [128, 8, 512], BF16)
                    av["k2f"] = t2a("k2f", [128, 8, 512], BF16)
                    av["v2t"] = t2a("v2t", [128, 4, 4, 257], BF16)
                    av["pT2"] = t2a("pT2", [128, 2, 128], BF16)
                    av["linv2"] = t2a("linv2", [128, 4], F32)
                    av["o2tok"] = t2a("o2tok", [128, 2, C], BF16)
                    av["o2f"] = t2a("o2f", [128, 8, 512], BF16)
                    av["xres2"] = t2a("xres2", [128, 4, C], F32)
                    av["xrebf"] = t2a("xrebf", [128, 4, C], BF16)
                    av["htok"] = t2a("htok", [128, 2, C], BF16)
                    av["y1lo"] = av["k2f"]
                    av["gt4"] = t2a("gt4", [128, 4, 8], F32)
                    av["rt1"] = t2a("rt1", [128, 4, 8], F32)
                    av["rt2"] = t2a("rt2", [128, 4, 8], F32)
                    av["mx4"] = t2a("mx4", [128, 4], F32)
                    av["sm4"] = t2a("sm4", [128, 4], F32)
                    av["i9"] = t2a("i9", [128, 8], F32)
                    av["hfull"] = t2a("hfull", [128, C], F32)
                    av["lotok"] = t2a("lotok", [128, 2, C], BF16)
                    p = Plan(counts)
                    gen_phase2a(p, A, cst, c2, av, drs, M)
                    emit(nc, blk, p, sems, wmarks)
                    counts = p.cnt

                with contextlib.ExitStack() as S2b:
                    t2b = lambda *a: S2b.enter_context(nc.sbuf_tensor(*a))
                    bv2 = {}
                    bv2["wc"] = t2b("wc", [128, 2, 8, C], BF16)
                    c2["coefs"] = t2b("coefs", [9, NTOK2], BF16)
                    bv2["rtb"] = t2b("rtb", [128, 8], F32)
                    bv2["ztile"] = t2b("ztile", [128, 8, C], F32)
                    bv2["y1tok"] = t2b("y1tok", [128, 2, C], BF16)
                    bv2["h2"] = t2b("h2", [128, 4, C], F32)
                    bv2["x1rl"] = t2b("x1rl", [128, 2, C], BF16)
                    bv2["outtok"] = t2b("outtok", [128, 2, C], F32)
                    bv2["zps"] = [S2b.enter_context(nc.psum_tensor(f"zps{i}", [128, 512], F32)) for i in range(4)]

                    p = Plan(counts)
                    gen_route_finish(p, A, cst, c2, bv2, drs, M)
                    gen_phase2bc(p, A, cst, c2, bv2, drs, M)
                    emit(nc, blk, p, sems, wmarks)
                    counts = p.cnt

    return nc


def gen_consts(p, A, cst, M):
    ch = "Cc"
    p.dma("gpsimd", ch, lambda g: g.dma_start(out=cst["msk"][:, :, :], in_=A["cmask"].transpose([1, 0, 2])))
    p.dma("gpsimd", ch, lambda g: g.dma_start(out=cst["ones"][:, :], in_=A["onesbf"]))
    M["consts"] = p.c(ch)
    p.v(lambda v: v.memset(cst["negb"][:, :], NEGB))
    M["negb"] = p.v(lambda v: v.memset(cst["eps"][:, :], EPS))


def gen_combine(p, A, cst, cv, drs, M):
    w1p, w1T, w2s = cv["w1p"], cv["w1T"], cv["w2s"]
    b1s, b2s, wcev, bev = cv["b1s"], cv["b2s"], cv["wcev"], cv["bev"]
    ps_c, ps_cb = cv["ps_c"], cv["ps_cb"]

    p.dma("gpsimd", "Ccm", lambda g: g.dma_start(
        out=w2s[:, :, :], in_=A["w2_own"].rearrange("(f p) d -> p f d", p=128)))
    p.dma("gpsimd", "Ccm", lambda g: g.dma_start(
        out=cv["b1tmp"][:, :], in_=A["b1_own"].rearrange("(f p) -> f p", p=128)))
    p.dma("gpsimd", "Ccm", lambda g: g.dma_start(out=b2s[:, :], in_=A["b2_own"].unsqueeze(0)))
    mCM = p.c("Ccm")
    p.dma("sync", "Cb1s", lambda s: s.dma_start(
        out=b1s[:, :], in_=cv["b1tmp"][:, :], transpose=True), waits=[("Ccm", mCM)])

    tr_mark = {}
    for cc in range(8):
        chp = f"Cw1p{cc % 2}"
        cht = "Cw1T" if cc % 2 == 0 else "Cw1U"
        mld = p.dma("gpsimd", chp, lambda g, cc=cc: g.dma_start(
            out=w1p[:, cc % 2, :], in_=A["w1_own"].rearrange("(c p) f -> p c f", p=128)[:, cc, :]),
            waits=([(cht, tr_mark[cc - 2])] if cc >= 2 else []))
        for f in range(32):
            p.dma("sync", cht, lambda s, cc=cc, f=f: s.dma_start(
                out=w1T[:, f, 128 * cc:128 * (cc + 1)],
                in_=w1p[:, cc % 2, 128 * f:128 * (f + 1)], transpose=True),
                waits=[(chp, mld)])
        tr_mark[cc] = p.c(cht)
    mW1T = p.c("Cw1T")
    mW1U = p.c("Cw1U")

    ev_mark = [0, 0]
    for di in range(8):
        for ch in range(2):
            pb = (di * 2 + ch) % 2
            for f in range(32):
                w = [("Cw1T", mW1T), ("Cw1U", mW1U), ("Ccm", mCM),
                     ("V", ev_mark[pb])] if f == 0 else []
                p.t(lambda t, di=di, ch=ch, f=f, pb=pb: t.matmul(
                    ps_c[pb][:, :], w1T[:, f, 128 * di:128 * (di + 1)],
                    w2s[:, f, 512 * ch:512 * (ch + 1)], start=(f == 0), stop=(f == 31)),
                    waits=w)
            mC = p.c("T")
            ev_mark[pb] = p.v(lambda v, pb=pb: v.tensor_copy(wcev[:, pb, :], ps_c[pb][:, :]),
                              waits=[("T", mC)])
            p.dma("gpsimd", "Cwc", lambda g, di=di, ch=ch, pb=pb: g.dma_start(
                out=drs["wc_in"].ap()[128 * di:128 * (di + 1), 512 * ch:512 * (ch + 1)],
                in_=wcev[:, pb, :]), waits=[("V", ev_mark[pb])])
    for ch in range(2):
        for f in range(32):
            w = [("V", p.c("V")), ("Ccm", mCM), ("Cb1s", 16)] if f == 0 else []
            p.t(lambda t, ch=ch, f=f: t.matmul(
                ps_cb[:, :], b1s[:, f:f + 1], w2s[:, f, 512 * ch:512 * (ch + 1)],
                start=(f == 0), stop=(f == 31)), waits=w)
        mC = p.c("T")
        mv = p.v(lambda v, ch=ch: v.tensor_tensor(
            bev[:, ch, :], ps_cb[:, :], b2s[:, 512 * ch:512 * (ch + 1)], op=OP.add),
            waits=[("T", mC)])
        p.dma("gpsimd", "Cwc", lambda g, ch=ch: g.dma_start(
            out=drs["bias_me"].ap()[512 * ch:512 * (ch + 1)].unsqueeze(0), in_=bev[:, ch, :]),
            waits=[("V", mv)])
    mWC = p.c("Cwc")
    p.ev("gpsimd", lambda g: g.collective_compute(
        "AllGather", OP.bypass, replica_groups=[list(range(NCORES))],
        ins=[drs["wc_in"].ap().opt()], outs=[drs["wcs"].ap().opt()]),
        waits=[("Cwc", mWC)], inc=("CC", 1))
    p.ev("gpsimd", lambda g: g.collective_compute(
        "AllGather", OP.bypass, replica_groups=[list(range(NCORES))],
        ins=[drs["bias_me"].ap().opt()], outs=[drs["bias_ag"].ap().opt()]), inc=("CC", 1))
    M["ag_done"] = p.c("CC")


def gen_phase1(p, A, cst, bv, drs, M):
    x_f, k_f, v_t, q_fb = bv["x_f"], bv["k_f"], bv["v_t"], bv["q_fb"]
    xbf, wbuf, wv, wp = bv["xbf"], bv["wbuf"], bv["wv"], bv["wp"]
    pT, o_tok, o_f, linv = bv["pT"], bv["o_tok"], bv["o_f"], bv["linv"]
    xres, x1b, rstd = bv["xres"], bv["x1b"], bv["rstd"]
    s1, s2, mean, m2t, var, junk = bv["s1"], bv["s2"], bv["mean"], bv["m2t"], bv["var"], bv["junk"]
    ps_mm, ps_s, ps_o = bv["ps_mm"], bv["ps_s"], bv["ps_o"]
    ones, negb, eps = cst["ones"], cst["negb"], cst["eps"]
    batt, bproj = bv["batt"], bv["bproj"]
    tri = cst["msk"][:, 0, :]
    a2a_in = drs["a2a_in"]

    p.dma("gpsimd", "Cc", lambda g: g.dma_start(out=batt[:, :], in_=A["b_attn"].unsqueeze(0)))
    p.dma("gpsimd", "Cc", lambda g: g.dma_start(out=bproj[:, :], in_=A["b_proj"].unsqueeze(0)))
    M["consts"] = p.c("Cc")
    mWP = p.dma("gpsimd", "Cwp", lambda g: g.dma_start(
        out=wp[:, :, :], in_=A["w_proj"].rearrange("(k p) d -> p k d", p=128)))

    w_free = {}
    ev_psmm = [("V", 0), ("V", 0)]
    v_pso = [0] * 4
    qev = {}
    ss_free = [0, 0]
    pT_free = [0, 0]
    oe_last = [0] * 4
    linv_rd = {}
    of_cnt = [0]
    xr_free = [0, 0, 0, 0]
    x1_cnt = [0, 0]
    sq_last = [0]
    stat_rd = [0]
    widx = [0]
    xf_free = [0]
    qk_prev = [0]
    proj_prev = [0]
    vpass_done = [0]

    def wchunk_load(dk):
        i = widx[0]
        ch = f"Cw{i % 4}"
        m = p.dma("gpsimd", ch, lambda g, dk=dk, i=i: g.dma_start(
            out=wbuf[:, i % 4, :, :],
            in_=A["w_attn"][:, 128 * dk:128 * (dk + 1)].rearrange("(k p) f -> p k f", p=128)),
            waits=[("T", w_free.get(i - 4, 0))])
        widx[0] += 1
        return i, ch, m

    def big_mm(pb, lhs_list, rhs_list, bias_pair, extra_waits):
        for i in range(len(lhs_list)):
            w = (list(extra_waits) + [ev_psmm[pb]]) if i == 0 else []
            st = (i == 0)
            sp = (i == len(lhs_list) - 1 and bias_pair is None)
            p.t(lambda t, i=i, st=st, sp=sp, pb=pb: t.matmul(
                ps_mm[pb][:, :], lhs_list[i](), rhs_list[i](), start=st, stop=sp), waits=w)
        if bias_pair is not None:
            lf, rf = bias_pair
            p.t(lambda t, pb=pb: t.matmul(ps_mm[pb][:, :], lf(), rf(), start=False, stop=True),
                waits=[("Cc", M["consts"])])
        return p.c("T")

    for s in range(SEQ):
        for tl in range(16):
            gt16 = 16 * s + tl
            slot = gt16 % 4
            waits = []
            if gt16 >= 4:
                waits.append((f"Ct{slot}", 128 * (gt16 // 4)))
            if s == 1 and tl < 4:
                waits.append(("T", xf_free[0]))
            mxl = p.dma("gpsimd", f"Cx{slot}", lambda g, s=s, tl=tl, slot=slot: g.dma_start(
                out=xbf[:, slot, :], in_=A["x"][s, 128 * tl:128 * (tl + 1), :]), waits=waits)
            for j in range(8):
                p.dma("sync", f"Ct{slot}", lambda y, tl=tl, j=j, slot=slot: y.dma_start(
                    out=x_f[:, j, 128 * tl:128 * (tl + 1)],
                    in_=xbf[:, slot, 128 * j:128 * (j + 1)], transpose=True),
                    waits=[(f"Cx{slot}", mxl)])

        def xf_wait(tl, s=s):
            gt16 = 16 * s + tl
            return (f"Ct{gt16 % 4}", 128 * (gt16 // 4 + 1))

        for dk8 in range(8):
            wi, ch, wm = wchunk_load(8 + dk8)
            for tb in range(4):
                pb = (dk8 * 4 + tb) % 2
                mm = big_mm(
                    pb,
                    [(lambda cc=cc, wi=wi: wbuf[:, wi % 4, cc, :]) for cc in range(8)],
                    [(lambda cc=cc, tb=tb: x_f[:, cc, 512 * tb:512 * (tb + 1)]) for cc in range(8)],
                    ((lambda dk8=dk8: batt[0:1, 1024 + 128 * dk8:1024 + 128 * (dk8 + 1)]),
                     (lambda: ones[0:1, 0:512])),
                    [(ch, wm)] + [xf_wait(4 * tb + jj) for jj in range(4)])
                ev_psmm[pb] = ("V", p.v(lambda v, dk8=dk8, tb=tb, pb=pb: v.tensor_copy(
                    k_f[:, dk8, 512 * tb:512 * (tb + 1)], ps_mm[pb][:, :]), waits=[("T", mm)]))
            w_free[wi] = mm

        mwv = []
        for ch_ in range(2):
            mwv.append(p.dma("gpsimd", f"Cwv{ch_}", lambda g, ch_=ch_: g.dma_start(
                out=wv[:, ch_, :, :],
                in_=A["w_attn"][:, 2 * C + 512 * ch_:2 * C + 512 * (ch_ + 1)]
                .rearrange("(k p) f -> p k f", p=128)),
                waits=[("T", vpass_done[0])]))
        if s == 0:
            p.v(lambda v: v.memset(v_t[:, :, :, 64:65], 1.0))
        mones = p.c("V")
        for tt in range(16):
            for ch_ in range(2):
                pb = (tt * 2 + ch_) % 2
                mm = big_mm(
                    pb,
                    [(lambda cc=cc, tt=tt: x_f[:, cc, 128 * tt:128 * (tt + 1)]) for cc in range(8)],
                    [(lambda cc=cc, ch_=ch_: wv[:, ch_, cc, :]) for cc in range(8)],
                    ((lambda: ones[0:1, 0:128]),
                     (lambda ch_=ch_: batt[0:1, 2 * C + 512 * ch_:2 * C + 512 * (ch_ + 1)])),
                    [(f"Cwv{ch_}", mwv[ch_]), xf_wait(tt), ("V", mones)])
                ev_psmm[pb] = ("V", p.v(lambda v, tt=tt, ch_=ch_, pb=pb: v.tensor_copy(
                    v_t[:, tt, 8 * ch_:8 * (ch_ + 1), 0:64], ps_mm[pb][:, :]), waits=[("T", mm)]))
        vpass_done[0] = p.c("T")

        qpb = [ps_mm[0], ps_mm[1]] + [ps_o[i] for i in range(4)]

        def qpass(q, defer):
            if not defer:
                for dq in range(8):
                    wi, ch, wm = wchunk_load(dq)
                    pb = dq % 2
                    mm = big_mm(
                        pb,
                        [(lambda cc=cc, wi=wi: wbuf[:, wi % 4, cc, :]) for cc in range(8)],
                        [(lambda cc=cc, q=q: x_f[:, cc, 512 * q:512 * (q + 1)]) for cc in range(8)],
                        ((lambda dq=dq: batt[0:1, 128 * dq:128 * (dq + 1)]),
                         (lambda: ones[0:1, 0:512])),
                        [(ch, wm)])
                    w_free[wi] = mm
                    ev_psmm[pb] = ("V", p.v(lambda v, dq=dq, pb=pb: v.tensor_copy(
                        q_fb[:, dq, :], ps_mm[pb][:, :]),
                        waits=[("T", mm), ("T", qk_prev[0])]))
                qev[q] = ("V", p.c("V"))
                return
            mmk = {}

            def qmm(dq):
                wi, ch, wm = wchunk_load(dq)
                j = dq % 6
                bank = qpb[j]
                w0 = [ev_psmm[j]] if j < 2 else \
                    [("Sc", oe_last[j - 2]), ("V", v_pso[j - 2])]
                for cc in range(8):
                    p.t(lambda t, cc=cc, wi=wi, bank=bank, q=q: t.matmul(
                        bank[:, :], wbuf[:, wi % 4, cc, :],
                        x_f[:, cc, 512 * q:512 * (q + 1)],
                        start=(cc == 0), stop=False),
                        waits=(w0 + [(ch, wm)]) if cc == 0 else [])
                p.t(lambda t, dq=dq, bank=bank: t.matmul(
                    bank[:, :], batt[0:1, 128 * dq:128 * (dq + 1)], ones[0:1, 0:512],
                    start=False, stop=True), waits=[("Cc", M["consts"])])
                mm = p.c("T")
                w_free[wi] = mm
                mmk[dq] = mm

            def qevac(dq):
                j = dq % 6
                bank = qpb[j]
                m = p.s(lambda sc, dq=dq, bank=bank: sc.activation(
                    q_fb[:, dq, :], bank[:, :], AF.Copy),
                    waits=[("T", mmk[dq]), ("T", qk_prev[0])])
                if j < 2:
                    ev_psmm[j] = ("Sc", m)
                else:
                    oe_last[j - 2] = m
                return m

            for dq in range(6):
                qmm(dq)

            def rest():
                qevac(0)
                qmm(6)
                qevac(1)
                qmm(7)
                m = 0
                for dq in range(2, 8):
                    m = qevac(dq)
                qev[q] = ("Sc", m)
            qrest[0] = rest

        qrest = [None]
        for qb in range(4):
            if qb == 0:
                qpass(0, defer=False)

            nkt = 4 * qb + 4
            for h in range(NH):
                po = 64 * (h % 2)
                hc = h // 2

                def emit_pv(kt, me, h=h, qb=qb):
                    rel = kt - 4 * qb
                    wpv = [("Sc", me)]
                    if rel >= 0:
                        mmk = p.v(lambda v, kt=kt, rel=rel: v.tensor_tensor(
                            pT[:, kt % 2, 128 * rel:128 * (rel + 1)],
                            pT[:, kt % 2, 128 * rel:128 * (rel + 1)], tri, op=OP.mult),
                            waits=[("Sc", me), ("Cc", M["consts"])])
                        wpv.append(("V", mmk))
                    for qt in range(max(0, rel), 4):
                        if kt == 0:
                            wpv.append(("Sc", oe_last[qt]))
                        p.t(lambda t, kt=kt, qt=qt, h=h, qb=qb: t.matmul(
                            ps_o[qt][:, 0:65], pT[:, kt % 2, 128 * qt:128 * (qt + 1)],
                            v_t[:, kt, h, :], start=(kt == 0), stop=(kt == 4 * qb + qt)),
                            waits=wpv)
                        wpv = []
                    pT_free[kt % 2] = p.c("T")

                pend = None
                for kt in range(nkt):
                    p.t(lambda t, po=po, hc=hc, kt=kt: t.matmul(
                        ps_s[kt % 2][:, :], k_f[po:po + 64, hc, 128 * kt:128 * (kt + 1)],
                        q_fb[po:po + 64, hc, :], start=True, stop=True),
                        waits=[("Sc", ss_free[kt % 2]), qev[qb]])
                    mqk = p.c("T")
                    vs = max(0, kt - 4 * qb) * 128
                    me = p.s(lambda sc, kt=kt, vs=vs: sc.activation(
                        pT[:, kt % 2, vs:512], ps_s[kt % 2][:, vs:512], AF.Exp,
                        bias=negb[:, :], scale=0.125),
                        waits=[("T", mqk), ("T", pT_free[kt % 2]), ("V", M["negb"])])
                    ss_free[kt % 2] = me
                    if pend is not None:
                        emit_pv(*pend)
                    pend = (kt, me)
                emit_pv(*pend)
                for qt in range(4):
                    mrc = p.v(lambda v, h=h, qt=qt: v.reciprocal(
                        linv[:, h, qt:qt + 1], ps_o[qt][:, 64:65]),
                        waits=[("T", pT_free[(4 * qb + qt) % 2]),
                               ("Sc", linv_rd.get((h, qt), 0))])
                    v_pso[qt] = mrc
                    oe_last[qt] = p.s(lambda sc, h=h, qt=qt: sc.activation(
                        o_tok[:, qt, h, :], ps_o[qt][:, 0:64], AF.Copy,
                        scale=linv[:, h, qt:qt + 1]),
                        waits=[("V", mrc), ("Cof", 512 * of_cnt[0])])
                    linv_rd[(h, qt)] = oe_last[qt]
            qk_prev[0] = p.c("T")
            moall = p.c("Sc")

            for qt in range(4):
                for j in range(8):
                    p.dma("sync", "Cof", lambda y, qt=qt, j=j: y.dma_start(
                        out=o_f[:, j, 128 * qt:128 * (qt + 1)],
                        in_=o_tok[:, qt, 2 * j:2 * j + 2, :].rearrange("p a b -> p (a b)"),
                        transpose=True),
                        waits=[("Sc", moall), ("T", proj_prev[0])])
            of_cnt[0] += 1
            mOF = 512 * of_cnt[0]

            for tb in range(4):
                gt = 4 * qb + tb
                chx = f"Cxr{tb}"
                mxr = p.dma("gpsimd", chx, lambda g, s=s, gt=gt, tb=tb: g.dma_start(
                    out=xres[:, tb, :], in_=A["x"][s, 128 * gt:128 * (gt + 1), :]),
                    waits=[("V", xr_free[tb])])
                for ch_ in range(2):
                    pb = ch_
                    mm = big_mm(
                        pb,
                        [(lambda cc=cc, tb=tb: o_f[:, cc, 128 * tb:128 * (tb + 1)]) for cc in range(8)],
                        [(lambda cc=cc, ch_=ch_: wp[:, cc, 512 * ch_:512 * (ch_ + 1)]) for cc in range(8)],
                        ((lambda: ones[0:1, 0:128]),
                         (lambda ch_=ch_: bproj[0:1, 512 * ch_:512 * (ch_ + 1)])),
                        [("Cof", mOF), ("Cwp", mWP)])
                    mres = p.v(lambda v, tb=tb, ch_=ch_, pb=pb: v.tensor_tensor(
                        xres[:, tb, 512 * ch_:512 * (ch_ + 1)], ps_mm[pb][:, :],
                        xres[:, tb, 512 * ch_:512 * (ch_ + 1)], op=OP.add),
                        waits=[("T", mm), (chx, mxr)])
                    ev_psmm[pb] = ("V", mres)
                p.v(lambda v, tb=tb: (v.tensor_reduce(
                    s1[:, tb:tb + 1], xres[:, tb, :], axis=AX, op=OP.add), v.drain())[-1],
                    waits=[("V", stat_rd[0])])
                sq_last[0] = p.s(lambda sc, tb=tb: (sc.activation(
                    junk[:, :], xres[:, tb, :], AF.Square,
                    accum_out=s2[:, tb:tb + 1]), sc.drain())[-1],
                    waits=[("V", mres), ("V", stat_rd[0])])
            if qb < 3:
                qpass(qb + 1, defer=True)
            mrd = p.c("V")
            p.v(lambda v: (v.tensor_scalar(mean[:, :], s1[:, :], 1.0 / C, None, op0=OP.mult),
                           v.drain())[-1])
            p.v(lambda v: (v.tensor_tensor(m2t[:, :], mean[:, :], mean[:, :], op=OP.mult),
                           v.drain())[-1])
            mm2 = p.v(lambda v: (v.scalar_tensor_tensor(
                var[:, :], s2[:, :], 1.0 / C, m2t[:, :], op0=OP.mult, op1=OP.subtract),
                v.drain())[-1], waits=[("Sc", sq_last[0])])
            mvar = p.v(lambda v: (v.tensor_scalar(var[:, :], var[:, :], EPS, None, op0=OP.add),
                                  v.drain())[-1])
            msr = p.s(lambda sc: (sc.activation(rstd[:, :], var[:, :], AF.Sqrt, scale=1.0),
                                  sc.drain())[-1],
                      waits=[("V", mvar)])
            p.v(lambda v: (v.reciprocal(rstd[:, :], rstd[:, :]), v.drain())[-1],
                waits=[("Sc", msr)])
            p.v(lambda v: (v.tensor_tensor(m2t[:, :], rstd[:, :], rstd[:, :], op=OP.mult),
                           v.drain())[-1])
            p.v(lambda v: (v.tensor_tensor(m2t[:, :], m2t[:, :], var[:, :], op=OP.mult),
                           v.drain())[-1])
            p.v(lambda v: (v.tensor_scalar(m2t[:, :], m2t[:, :], -0.5, 1.5, op0=OP.mult, op1=OP.add),
                           v.drain())[-1])
            stat_rd[0] = p.v(lambda v: (v.tensor_tensor(rstd[:, :], rstd[:, :], m2t[:, :], op=OP.mult),
                                        v.drain())[-1])
            if qb < 3:
                qrest[0]()
            for tb in range(4):
                gt = 4 * qb + tb
                chs = f"Cx1s{tb % 2}"
                mln = p.v(lambda v, tb=tb: v.tensor_scalar(
                    x1b[:, tb % 2, :], xres[:, tb, :], mean[:, tb:tb + 1], rstd[:, tb:tb + 1],
                    op0=OP.subtract, op1=OP.mult),
                    waits=[(chs, 16 * x1_cnt[tb % 2])])
                xr_free[tb] = mln
                shard, off = gt // 2, 128 * (gt % 2)
                p.dma("scalar", chs, lambda sc, shard=shard, s=s, off=off, tb=tb: sc.dma_start(
                    out=a2a_in.ap()[shard, s, off:off + 128, :], in_=x1b[:, tb % 2, :]),
                    waits=[("V", mln)])
                x1_cnt[tb % 2] += 1
            proj_prev[0] = p.c("T")
        xf_free[0] = p.c("T")




def gen_phase2a_prep(p, A, cst, c2, av, drs, M):
    ones = cst["ones"]
    wiT_sp = drs["wiT_sp"]

    def lg(fn, waits=()):
        return p.dma("gpsimd", "Ccg", fn, waits=waits)
    lg(lambda g: g.dma_start(out=c2["bpat"][:, :], in_=A["bpatf"]))
    lg(lambda g: g.dma_start(out=c2["bpatT"][:, :], in_=A["bpatT"]))
    lg(lambda g: g.dma_start(out=c2["btri"][:, :], in_=A["btri16"]))
    lg(lambda g: g.dma_start(out=c2["idf"][:, :], in_=A["identf"]))
    lg(lambda g: g.dma_start(out=c2["rke"][:, :, :], in_=A["rankexcl"]))
    lg(lambda g: g.dma_start(out=c2["iota"][:, :], in_=A["iota8"]))
    mg1 = p.dma("gpsimd", "Cgt0", lambda g: g.dma_start(
        out=c2["gtmp"][0:8, :], in_=A["ln1_g"].rearrange("(k p) -> k p", p=128)))
    p.t(lambda t: t.transpose(av["ps2_s"][1][:, 0:16], c2["gtmp"][:, :], c2["idf"][0:16, 0:16]),
        waits=[("Cgt0", mg1), ("Ccg", p.c("Ccg"))])
    mgt = p.c("T")
    M["_gfev"] = p.v(lambda v: (v.tensor_copy(c2["gf"][:, :], av["ps2_s"][1][:, 0:16]),
                                v.drain())[-1], waits=[("T", mgt)])
    mg2 = p.dma("gpsimd", "Cgt1", lambda g: g.dma_start(
        out=c2["gtmpb"][0:8, :], in_=A["ln1_b"].rearrange("(k p) -> k p", p=128)))
    p.dma("sync", "Cgt1", lambda y: y.dma_start(out=c2["bfb"][:, :], in_=c2["gtmpb"][:, :],
                                                transpose=True), waits=[("Cgt1", mg2)])
    lg(lambda g: g.dma_start(out=c2["g2b"][:, :], in_=A["ln2_g"].unsqueeze(0).broadcast_to([128, C])))
    lg(lambda g: g.dma_start(out=c2["b2b"][:, :], in_=A["ln2_b"].unsqueeze(0).broadcast_to([128, C])))
    lg(lambda g: g.dma_start(out=c2["wgf"][:, :, :], in_=A["wg"].rearrange("(k p) e -> p k e", p=128)))
    lg(lambda g: g.dma_start(out=c2["wgp"][:, :, :], in_=A["wg"].rearrange("(k p) e -> p k e", p=128)))
    lg(lambda g: g.dma_start(out=c2["inb"][:, :], in_=A["in_b"].unsqueeze(0)))
    lg(lambda g: g.dma_start(out=c2["outb"][:, :], in_=A["out_b"].unsqueeze(0)))
    lg(lambda g: g.dma_start(out=c2["biasm"][8:9, :], in_=A["ln1_b"].unsqueeze(0)))
    lg(lambda g: g.dma_start(out=c2["biasm"][0:8, :], in_=drs["bias_ag"].ap()),
       waits=[("CC", M["ag_done"])])
    mCG = p.c("Ccg")

    for cc in range(8):
        p.v(lambda v, cc=cc: v.tensor_scalar(
            c2["wgf"][:, cc, :], c2["wgf"][:, cc, :], c2["gf"][:, cc:cc + 1], None,
            op0=OP.mult), waits=([("Ccg", mCG), ("V", M["_gfev"])] if cc == 0 else ()))
    p.v(lambda v: v.drain())
    p.v(lambda v: (v.tensor_copy(c2["wgh"][:, :, :], c2["wgf"][:, :, :]), v.drain())[-1])
    p.v(lambda v: (v.scalar_tensor_tensor(
        c2["wgl"][:, :, :], c2["wgh"][:, :, :], -1.0, c2["wgf"][:, :, :],
        op0=OP.mult, op1=OP.add), v.drain())[-1])

    for cc in range(8):
        p.t(lambda t, cc=cc: t.matmul(
            av["ps2_s"][0][0:1, 0:8], c2["bfb"][:, cc:cc + 1], c2["wgp"][:, cc, :],
            start=(cc == 0), stop=(cc == 7)),
            waits=[("Ccg", mCG), ("Cgt1", p.c("Cgt1"))] if cc == 0 else ())
    mbw = p.c("T")
    M["_mbwe"] = p.v(lambda v: (v.tensor_copy(c2["bwgb"][:, :], av["ps2_s"][0][0:1, 0:8]),
                                 v.drain())[-1], waits=[("T", mbw)])

    wtt_mark = [0, 0]
    wsp_cnt = [0, 0]
    for dd in range(24):
        chl = f"Cwsp{dd % 2}"
        mld = p.dma("gpsimd", chl, lambda g, dd=dd: g.dma_start(
            out=av["wtmp"][:, dd % 2, :], in_=A["in_w"][128 * dd:128 * (dd + 1), :]),
            waits=[(f"Cwtt{dd % 2}", wtt_mark[dd % 2])])
        for j in range(8):
            p.dma("sync", f"Cwtt{dd % 2}", lambda y, dd=dd, j=j: y.dma_start(
                out=av["wiTt"][:, dd % 2, j, :],
                in_=av["wtmp"][:, dd % 2, 128 * j:128 * (j + 1)], transpose=True),
                waits=[(chl, mld), (f"Cwisp{dd % 2}", wsp_cnt[dd % 2])])
        wtt_mark[dd % 2] = p.c(f"Cwtt{dd % 2}")
        wsp_cnt[dd % 2] = p.dma("gpsimd", f"Cwisp{dd % 2}", lambda g, dd=dd: g.dma_start(
            out=wiT_sp.ap()[:, :, 128 * dd:128 * (dd + 1)], in_=av["wiTt"][:, dd % 2, :, :]),
            waits=[(f"Cwtt{dd % 2}", wtt_mark[dd % 2])])
    for dd in range(8):
        chl = f"Cwsp{dd % 2}"
        mld = p.dma("gpsimd", chl, lambda g, dd=dd: g.dma_start(
            out=av["wtmp"][:, dd % 2, :], in_=A["out_w"][128 * dd:128 * (dd + 1), :]),
            waits=[(f"Cwtt{dd % 2}", wtt_mark[dd % 2])])
        for j in range(8):
            p.dma("sync", f"Cwtt{dd % 2}", lambda y, dd=dd, j=j: y.dma_start(
                out=av["woT"][:, j, 128 * dd:128 * (dd + 1)],
                in_=av["wtmp"][:, dd % 2, 128 * j:128 * (j + 1)], transpose=True),
                waits=[(chl, mld)])
        wtt_mark[dd % 2] = p.c(f"Cwtt{dd % 2}")
    M["mWOT"] = p.c("Cwtt0")
    M["mWOT1"] = p.c("Cwtt1")
    M["mWISP"] = (p.c("Cwisp0"), p.c("Cwisp1"))
    M["mCG"] = p.c("Ccg")
    M["mbwe"] = M.get("_mbwe", 0)


def gen_phase2a(p, A, cst, c2, av, drs, M):
    ones, negb = cst["ones"], cst["negb"]
    bd16 = cst["msk"][:, 2, :]
    tri_s = cst["msk"][:, 1, :]
    y1f, cnts, gval, ohb = c2["y1f"], c2["cnts"], c2["gval"], c2["ohb"]
    carry = c2["carry"]
    wiT_sp, a2a_out = drs["wiT_sp"], drs["a2a_out"]
    mCG = M["mCG"]
    mWISP0, mWISP1 = M["mWISP"]
    mWOT, mWOT1 = M["mWOT"], M["mWOT1"]
    mbwe = M["_mbwe"]

    p.v(lambda v: (v.tensor_scalar(av["i9"][:, :], c2["iota"][:, :], 9.0, None,
                                   op0=OP.subtract), v.drain())[-1], waits=[("Ccg", mCG)])
    p.v(lambda v: v.memset(av["v2t"][:, :, :, 256:257], 1.0))
    p.v(lambda v: v.memset(av["linv2"][:, :], 1.0))
    mPRE = p.v(lambda v: (v.memset(c2["carry"][:, :], 0.0), v.drain())[-1])

    ev_ps = [("V", 0), ("V", 0)]
    ss_free = [0, 0]
    pT_free = [0, 0]
    oe_last = [0] * 4
    v_ps2o = [0] * 4
    linv_rd = {}
    wi_free = {}
    xf_mms = [0]
    xrb_free = [0, 0, 0, 0]
    xr2_free = [0, 0, 0, 0]
    proj_T = [0]
    stat_rd = [0]
    gates_T = [0]
    carryV = [mPRE]
    gsc_V = [0]
    widx = [0]
    ct_full = [p.c(f"Ct{i}") for i in range(2)]
    xload_mark = [0, 0, 0, 0]
    yh_free = [0, 0]
    yl_free = [0, 0]
    ot_free = [0, 0]
    cntev = {}
    mQKV_t = [("V", 0)]

    def big_mm2(pb, lhs_list, rhs_list, bias_pair, extra_waits):
        for i in range(len(lhs_list)):
            w = (list(extra_waits) + [ev_ps[pb]]) if i == 0 else []
            p.t(lambda t, i=i, pb=pb, st=(i == 0): t.matmul(
                av["ps2_mm"][pb][:, :], lhs_list[i](), rhs_list[i](),
                start=(i == 0), stop=False), waits=w)
        lf, rf = bias_pair
        p.t(lambda t, pb=pb: t.matmul(av["ps2_mm"][pb][:, :], lf(), rf(),
                                      start=False, stop=True))
        return p.c("T")

    bankap = [av["ps2_mm"][0], av["ps2_mm"][1]] + [av["ps2_o"][h] for h in range(4)]
    bank_wait = [None] * 6
    unit_mm = [0] * 24

    def front_loads(tnext):
        for bb in range(4):
            n0 = 32 * tnext + 8 * bb
            src_ap = a2a_out.ap()[:, :, n0:n0 + 8, :].transpose([2, 0, 1, 3])
            xload_mark[bb] = p.dma("gpsimd", f"Cxb{bb}", lambda g, src_ap=src_ap, bb=bb:
                g.dma_start(out=av["x1bf"][:, bb, :], in_=src_ap),
                waits=[("CC", M["a2a_done"]), (f"Ct{bb % 2}", ct_full[bb % 2])])

    def front_tr():
        for bb in range(4):
            sl = bb % 2
            for j in range(8):
                p.dma("sync", f"Ct{sl}", lambda y, bb=bb, j=j: y.dma_start(
                    out=av["x1f"][:, j, 128 * bb:128 * (bb + 1)],
                    in_=av["x1bf"][:, bb, 128 * j:128 * (j + 1)], transpose=True),
                    waits=[(f"Cxb{bb}", xload_mark[bb]), ("T", xf_mms[0])])
        ct_full[0] = p.c("Ct0")
        ct_full[1] = p.c("Ct1")
        return [("Ct0", ct_full[0]), ("Ct1", ct_full[1])]

    def bank_init():
        bank_wait[0] = [ev_ps[0]]
        bank_wait[1] = [ev_ps[1]]
        for h in range(4):
            bank_wait[2 + h] = [("Sc", oe_last[h]), ("V", v_ps2o[h])]

    def emit_unit(i, xf_marks):
        j = i % 6
        bank = bankap[j]
        w0 = list(bank_wait[j]) + list(xf_marks) + [("Ccg", mCG)]
        if i < 8 or i >= 16:
            dk = i if i < 8 else i - 8
            ii = widx[0]
            chw = f"Cw{ii % 2}"
            mw = p.dma("gpsimd", chw, lambda g, dk=dk, ii=ii: g.dma_start(
                out=av["wibuf"][:, ii % 2, :, :],
                in_=wiT_sp.ap()[:, :, 128 * dk:128 * (dk + 1)]),
                waits=[("Cwisp0", mWISP0), ("Cwisp1", mWISP1), ("T", wi_free.get(ii - 2, 0))])
            widx[0] += 1
            for cc in range(8):
                p.t(lambda t, cc=cc, ii=ii, bank=bank, st=(cc == 0): t.matmul(
                    bank[:, :], av["wibuf"][:, ii % 2, cc, :], av["x1f"][:, cc, 0:512],
                    start=(cc == 0), stop=False),
                    waits=(w0 + [(chw, mw)]) if cc == 0 else [])
            p.t(lambda t, dk=dk, bank=bank: t.matmul(
                bank[:, :], c2["inb"][0:1, 128 * dk:128 * (dk + 1)], ones[0:1, 0:512],
                start=False, stop=True))
            wi_free[ii] = p.c("T")
            unit_mm[i] = wi_free[ii]
        else:
            chv = (i - 8) // 4
            bb = (i - 8) % 4
            if bb == 0:
                mwv = p.dma("gpsimd", f"Cwv{chv}", lambda g, chv=chv: g.dma_start(
                    out=av["wvbuf"][:, 0, :, :],
                    in_=wiT_sp.ap()[:, :, 2 * C + 512 * chv:2 * C + 512 * (chv + 1)]),
                    waits=[("Cwisp0", mWISP0), ("Cwisp1", mWISP1), ("T", p.c("T"))])
                M[("mwv", chv)] = mwv
            mwv = M[("mwv", chv)]
            for cc in range(8):
                p.t(lambda t, cc=cc, bb=bb, bank=bank: t.matmul(
                    bank[:, :], av["x1f"][:, cc, 128 * bb:128 * (bb + 1)],
                    av["wvbuf"][:, 0, cc, :], start=(cc == 0), stop=False),
                    waits=(w0 + [(f"Cwv{chv}", mwv)]) if cc == 0 else [])
            p.t(lambda t, chv=chv, bank=bank: t.matmul(
                bank[:, :], ones[0:1, 0:128],
                c2["inb"][0:1, 2 * C + 512 * chv:2 * C + 512 * (chv + 1)],
                start=False, stop=True))
            unit_mm[i] = p.c("T")
        bank_wait[j] = None

    def emit_evac(i, extra=()):
        j = i % 6
        bank = bankap[j]
        if i < 8:
            m = p.s(lambda sc, i=i, bank=bank: sc.activation(
                av["q2f"][:, i, :], bank[:, :], AF.Copy),
                waits=[("T", unit_mm[i])] + list(extra))
        elif i < 16:
            chv = (i - 8) // 4
            bb = (i - 8) % 4
            p.s(lambda sc, chv=chv, bb=bb, bank=bank: sc.activation(
                av["v2t"][:, bb, 2 * chv, 0:256], bank[:, 0:256], AF.Copy),
                waits=[("T", unit_mm[i]), ("V", mPRE)] + list(extra))
            m = p.s(lambda sc, chv=chv, bb=bb, bank=bank: sc.activation(
                av["v2t"][:, bb, 2 * chv + 1, 0:256], bank[:, 256:512], AF.Copy))
        else:
            dk = i - 16
            m = p.s(lambda sc, dk=dk, bank=bank: sc.activation(
                av["k2f"][:, dk, :], bank[:, :], AF.Copy),
                waits=[("T", unit_mm[i])] + list(extra))
        bank_wait[j] = [("Sc", m)]
        return m

    x1_cnt2 = dict(M.get("x1_cnt2", {0: 0, 1: 0}))
    xt_prev = [x1_cnt2[0], x1_cnt2[1]]
    xf_marks0 = []
    for bb in range(4):
        n0 = 8 * bb
        sl = bb % 2
        src_ap = a2a_out.ap()[:, :, n0:n0 + 8, :].transpose([2, 0, 1, 3])
        mxl = p.dma("gpsimd", f"Cxb{bb}", lambda g, src_ap=src_ap, bb=bb: g.dma_start(
            out=av["x1bf"][:, bb, :], in_=src_ap),
            waits=[("CC", M["a2a_done"]), ("Cwtt0", mWOT), ("Cwtt1", mWOT1),
                   ("Cwisp0", mWISP0), ("Cwisp1", mWISP1)])
        for j in range(8):
            p.dma("sync", f"Ct{sl}", lambda y, bb=bb, sl=sl, j=j: y.dma_start(
                out=av["x1f"][:, j, 128 * bb:128 * (bb + 1)],
                in_=av["x1bf"][:, bb, 128 * j:128 * (j + 1)], transpose=True),
                waits=[(f"Cxb{bb}", mxl)])
    ct_full[0] = p.c("Ct0")
    ct_full[1] = p.c("Ct1")
    xf_marks0 = [("Ct0", ct_full[0]), ("Ct1", ct_full[1])]
    for dk in range(16):
        ii = widx[0]
        chw = f"Cw{ii % 2}"
        mw = p.dma("gpsimd", chw, lambda g, dk=dk, ii=ii: g.dma_start(
            out=av["wibuf"][:, ii % 2, :, :], in_=wiT_sp.ap()[:, :, 128 * dk:128 * (dk + 1)]),
            waits=[("Cwisp0", mWISP0), ("Cwisp1", mWISP1), ("T", wi_free.get(ii - 2, 0))])
        widx[0] += 1
        pb = dk % 2
        mm = big_mm2(
            pb,
            [(lambda cc=cc, ii=ii: av["wibuf"][:, ii % 2, cc, :]) for cc in range(8)],
            [(lambda cc=cc: av["x1f"][:, cc, 0:512]) for cc in range(8)],
            ((lambda dk=dk: c2["inb"][0:1, 128 * dk:128 * (dk + 1)]),
             (lambda: ones[0:1, 0:512])),
            [(chw, mw), ("Ccg", mCG)] + xf_marks0)
        wi_free[ii] = mm
        dst = av["q2f"] if dk < 8 else av["k2f"]
        ev_ps[pb] = ("V", p.v(lambda v, dst=dst, dk=dk, pb=pb: v.tensor_copy(
            dst[:, dk % 8, :], av["ps2_mm"][pb][:, :]), waits=[("T", mm)]))
    for chv in range(2):
        mwv = p.dma("gpsimd", f"Cwv{chv}", lambda g, chv=chv: g.dma_start(
            out=av["wvbuf"][:, 0, :, :],
            in_=wiT_sp.ap()[:, :, 2 * C + 512 * chv:2 * C + 512 * (chv + 1)]),
            waits=[("Cwisp0", mWISP0), ("Cwisp1", mWISP1), ("T", p.c("T"))])
        for bb in range(4):
            pb = (bb * 2 + chv) % 2
            mm = big_mm2(
                pb,
                [(lambda cc=cc, bb=bb: av["x1f"][:, cc, 128 * bb:128 * (bb + 1)]) for cc in range(8)],
                [(lambda cc=cc: av["wvbuf"][:, 0, cc, :]) for cc in range(8)],
                ((lambda: ones[0:1, 0:128]),
                 (lambda chv=chv: c2["inb"][0:1, 2 * C + 512 * chv:2 * C + 512 * (chv + 1)])),
                [(f"Cwv{chv}", mwv)] + xf_marks0)
            ev_ps[pb] = ("V", p.v(lambda v, bb=bb, chv=chv, pb=pb: v.tensor_copy(
                av["v2t"][:, bb, 2 * chv:2 * chv + 2, 0:256], av["ps2_mm"][pb][:, :]),
                waits=[("T", mm), ("V", mPRE)]))
    xf_mms[0] = p.c("T")
    mQKV_t[0] = ("V", p.c("V"))

    for tau in range(8):
        t0 = 512 * tau
        for bb in range(4):
            for h in range(4):
                sb = (bb * 4 + h) % 2
                for i in range(2):
                    p.t(lambda t, bb=bb, h=h, i=i, sb=sb: t.matmul(
                        av["ps2_s"][sb][:, 0:128], av["k2f"][:, 2 * h + i, 128 * bb:128 * (bb + 1)],
                        av["q2f"][:, 2 * h + i, 128 * bb:128 * (bb + 1)],
                        start=(i == 0), stop=(i == 1)),
                        waits=([("Sc", ss_free[sb]), mQKV_t[0], ("V", carryV[0])] if i == 0 else []))
                mqk = p.c("T")
                me = p.s(lambda sc, sb=sb: sc.activation(
                    av["pT2"][:, sb, :], av["ps2_s"][sb][:, 0:128], AF.Exp,
                    bias=negb[:, :], scale=0.0625),
                    waits=[("T", mqk), ("T", pT_free[sb])])
                ss_free[sb] = me
                mmk = p.v(lambda v, sb=sb: v.tensor_tensor(
                    av["pT2"][:, sb, :], av["pT2"][:, sb, :], bd16, op=OP.mult),
                    waits=[("Sc", me)])
                p.t(lambda t, bb=bb, h=h, sb=sb: t.matmul(
                    av["ps2_o"][h][:, 0:257], av["pT2"][:, sb, :], av["v2t"][:, bb, h, :],
                    start=True, stop=True),
                    waits=[("V", mmk), ("Sc", oe_last[h])])
                pT_free[sb] = p.c("T")
                mrc = p.v(lambda v, h=h: v.reciprocal(
                    av["linv2"][:, h:h + 1], av["ps2_o"][h][:, 256:257]),
                    waits=[("T", pT_free[sb]), ("Sc", linv_rd.get(h, 0))])
                v_ps2o[h] = mrc
                oe_last[h] = p.s(lambda sc, bb=bb, h=h: sc.activation(
                    av["o2tok"][:, bb % 2, 256 * h:256 * (h + 1)], av["ps2_o"][h][:, 0:256],
                    AF.Copy, scale=av["linv2"][:, h:h + 1]),
                    waits=[("V", mrc), (f"Cot{bb % 2}", ot_free[bb % 2])])
                linv_rd[h] = oe_last[h]
            mo = p.c("Sc")
            for j in range(8):
                p.dma("sync", f"Cot{bb % 2}", lambda y, bb=bb, j=j: y.dma_start(
                    out=av["o2f"][:, j, 128 * bb:128 * (bb + 1)],
                    in_=av["o2tok"][:, bb % 2, 128 * j:128 * (j + 1)], transpose=True),
                    waits=[("Sc", mo), ("T", proj_T[0])])
            ot_free[bb % 2] = p.c(f"Cot{bb % 2}")
        mOT = [("Cot0", p.c("Cot0")), ("Cot1", p.c("Cot1"))]

        if tau < 7:
            front_loads(tau + 1)

        for bb in range(4):
            chx = f"Cxr{bb}"
            mxr = p.dma("gpsimd", chx, lambda g, tau=tau, bb=bb: g.dma_start(
                out=av["xrebf"][:, bb, :],
                in_=a2a_out.ap()[:, :, 32 * tau + 8 * bb:32 * tau + 8 * (bb + 1), :]
                .transpose([2, 0, 1, 3])),
                waits=[("V", xrb_free[bb])])
            for chp in range(2):
                pb = chp
                mm = big_mm2(
                    pb,
                    [(lambda cc=cc, bb=bb: av["o2f"][:, cc, 128 * bb:128 * (bb + 1)]) for cc in range(8)],
                    [(lambda cc=cc, chp=chp: av["woT"][:, cc, 512 * chp:512 * (chp + 1)]) for cc in range(8)],
                    ((lambda: ones[0:1, 0:128]),
                     (lambda chp=chp: c2["outb"][0:1, 512 * chp:512 * (chp + 1)])),
                    list(mOT) + [("Cwtt0", mWOT), ("Cwtt1", mWOT1)])
                mev = p.v(lambda v, bb=bb, chp=chp, pb=pb: v.tensor_tensor(
                    av["xres2"][:, bb, 512 * chp:512 * (chp + 1)], av["ps2_mm"][pb][:, :],
                    av["xrebf"][:, bb, 512 * chp:512 * (chp + 1)], op=OP.add),
                    waits=[("T", mm), (chx, mxr)])
                ev_ps[pb] = ("V", mev)
            xrb_free[bb] = mev
            mres = mev
            p.v(lambda v, bb=bb: (v.tensor_reduce(
                c2["s1b"][:, bb:bb + 1], av["xres2"][:, bb, :], axis=AX, op=OP.add),
                v.drain())[-1], waits=[("V", stat_rd[0])])
            p.s(lambda sc, bb=bb: (sc.activation(
                c2["junk2"][:, :], av["xres2"][:, bb, :], AF.Square,
                accum_out=c2["s2b"][:, bb:bb + 1]), sc.drain())[-1],
                waits=[("V", mres), ("V", stat_rd[0])])
        proj_T[0] = p.c("T")
        mSQ = p.c("Sc")

        if tau < 7:
            xf_marks = front_tr()
            bank_init()
            for i in range(6):
                emit_unit(i, xf_marks)

        ln_chain(p, c2, mSQ)

        if tau < 7:
            for i in range(6, 22):
                emit_evac(i - 6)
                emit_unit(i, xf_marks)

        for bb in range(4):
            blkg = 4 * tau + bb
            mh = p.v(lambda v, bb=bb: v.tensor_scalar(
                av["hfull"][:, :], av["xres2"][:, bb, :],
                c2["meanb"][:, bb:bb + 1], c2["rstdb"][:, bb:bb + 1],
                op0=OP.subtract, op1=OP.mult),
                waits=[(f"Cyh{bb % 2}", yh_free[bb % 2]), (f"Cyl{bb % 2}", yl_free[bb % 2])])
            p.v(lambda v, bb=bb: v.tensor_copy(av["htok"][:, bb % 2, :], av["hfull"][:, :]))
            mlo = p.v(lambda v, bb=bb: v.scalar_tensor_tensor(
                av["lotok"][:, bb % 2, :], av["htok"][:, bb % 2, :], -1.0,
                av["hfull"][:, :], op0=OP.mult, op1=OP.add))
            xr2_free[bb] = mlo
            for j in range(8):
                p.dma("sync", f"Cyh{bb % 2}", lambda y, blkg=blkg, bb=bb, j=j: y.dma_start(
                    out=y1f[:, j, 128 * blkg:128 * (blkg + 1)],
                    in_=av["htok"][:, bb % 2, 128 * j:128 * (j + 1)], transpose=True),
                    waits=[("V", mlo), ("T", gates_T[0])])
                p.dma("sync", f"Cyl{bb % 2}", lambda y, bb=bb, j=j: y.dma_start(
                    out=av["y1lo"][:, j, 128 * bb:128 * (bb + 1)],
                    in_=av["lotok"][:, bb % 2, 128 * j:128 * (j + 1)], transpose=True),
                    waits=[("V", mlo), ("T", gates_T[0])])
            yh_free[bb % 2] = p.c(f"Cyh{bb % 2}")
            yl_free[bb % 2] = p.c(f"Cyl{bb % 2}")
        mYHL = [("Cyh0", p.c("Cyh0")), ("Cyh1", p.c("Cyh1")),
                ("Cyl0", p.c("Cyl0")), ("Cyl1", p.c("Cyl1"))]

        gev = []
        for tkb in range(4):
            blkg = 4 * tau + tkb
            first = True
            for cc in range(8):
                for (lhs, rhs) in ((lambda cc=cc, blkg=blkg: y1f[:, cc, 128 * blkg:128 * (blkg + 1)],
                                    lambda cc=cc: c2["wgh"][:, cc, :]),
                                   (lambda cc=cc, blkg=blkg: y1f[:, cc, 128 * blkg:128 * (blkg + 1)],
                                    lambda cc=cc: c2["wgl"][:, cc, :]),
                                   (lambda cc=cc, tkb=tkb: av["y1lo"][:, cc, 128 * tkb:128 * (tkb + 1)],
                                    lambda cc=cc: c2["wgh"][:, cc, :])):
                    w = (list(mYHL) + [("V", mbwe), ("Sc", ss_free[0]),
                         ("V", gev[-1] if gev else 0)]) if first else []
                    p.t(lambda t, lhs=lhs, rhs=rhs, first=first: t.matmul(
                        av["ps2_s"][0][:, 256:264], lhs(), rhs(),
                        start=first, stop=False), waits=w)
                    first = False
            p.t(lambda t: t.matmul(av["ps2_s"][0][:, 256:264], ones[0:1, 0:128],
                                   c2["bwgb"][0:1, :], start=False, stop=True))
            mg = p.c("T")
            gev.append(p.v(lambda v, tkb=tkb: v.tensor_copy(
                av["gt4"][:, tkb, :], av["ps2_s"][0][:, 256:264]), waits=[("T", mg)]))
        gates_T[0] = p.c("T")

        if tau < 7:
            gw = [("T", gates_T[0])]
            emit_evac(16, gw)
            emit_evac(17, gw)
            emit_unit(22, xf_marks)
            emit_unit(23, xf_marks)
            for i in range(18, 24):
                emit_evac(i, gw)
            xf_mms[0] = p.c("T")
            mQKV_t[0] = ("Sc", p.c("Sc"))

        for cc in range(8):
            p.v(lambda v, cc=cc, t0=t0: v.tensor_scalar(
                y1f[:, cc, t0:t0 + 512], y1f[:, cc, t0:t0 + 512],
                c2["gf"][:, cc:cc + 1], None, op0=OP.mult),
                waits=[("T", gates_T[0])] if cc == 0 else ())
        gsc_V[0] = p.c("V")
        M[("y1f_ready", tau)] = gsc_V[0]

        i9b = av["i9"][:, :].unsqueeze(1).broadcast_to([128, 4, 8])
        iob = c2["iota"][:, :].unsqueeze(1).broadcast_to([128, 4, 8])
        p.v(lambda v: (v.tensor_reduce(av["mx4"][:, :], av["gt4"][:, :, :], axis=AX,
                                       op=OP.max), v.drain())[-1])
        p.v(lambda v: (v.tensor_tensor(
            av["rt1"][:, :, :], av["gt4"][:, :, :],
            av["mx4"][:, :].unsqueeze(2).broadcast_to([128, 4, 8]), op=OP.subtract),
            v.drain())[-1])
        mex = p.s(lambda sc: (sc.activation(av["rt2"][:, :, :], av["rt1"][:, :, :], AF.Exp),
                              sc.drain())[-1], waits=[("V", p.c("V"))])
        p.v(lambda v: (v.tensor_reduce(av["sm4"][:, :], av["rt2"][:, :, :], axis=AX,
                                       op=OP.add), v.drain())[-1], waits=[("Sc", mex)])
        p.v(lambda v, tau=tau: (v.reciprocal(gval[:, 4 * tau:4 * tau + 4], av["sm4"][:, :]),
                                v.drain())[-1])
        p.v(lambda v: (v.tensor_tensor(
            av["rt1"][:, :, :], av["gt4"][:, :, :],
            av["mx4"][:, :].unsqueeze(2).broadcast_to([128, 4, 8]), op=OP.is_ge),
            v.drain())[-1])
        p.v(lambda v, i9b=i9b: (v.tensor_tensor(av["rt2"][:, :, :], av["rt1"][:, :, :],
                                                i9b, op=OP.mult), v.drain())[-1])
        p.v(lambda v: (v.tensor_scalar(av["rt2"][:, :, :], av["rt2"][:, :, :], 9.0, None,
                                       op0=OP.add), v.drain())[-1])
        p.v(lambda v: (v.tensor_reduce(av["mx4"][:, :], av["rt2"][:, :, :], axis=AX,
                                       op=OP.min), v.drain())[-1])
        mohv = p.v(lambda v, iob=iob, tau=tau: (v.tensor_tensor(
            ohb[:, 4 * tau:4 * tau + 4, :], iob,
            av["mx4"][:, :].unsqueeze(2).broadcast_to([128, 4, 8]), op=OP.is_equal),
            v.drain())[-1])
        for bb in range(4):
            blkg = 4 * tau + bb
            p.t(lambda t, blkg=blkg: t.matmul(
                av["ps2_s"][1][:, 0:8], tri_s, ohb[:, blkg, :], start=True, stop=False),
                waits=[("V", mohv), ("V", carryV[0]), ("Sc", ss_free[1])])
            p.t(lambda t: t.matmul(av["ps2_s"][1][:, 0:8], c2["bpat"][:, :], carry[:, :],
                                   start=False, stop=True))
            p.t(lambda t, blkg=blkg: t.matmul(
                av["ps2_s"][1][0:16, 8:16], c2["bpatT"][:, :], ohb[:, blkg, :],
                start=True, stop=True))
            mcnt = p.c("T")
            cntev[blkg] = p.v(lambda v, blkg=blkg: (v.tensor_copy(
                cnts[:, blkg, :], av["ps2_s"][1][:, 0:8]), v.drain())[-1],
                waits=[("T", mcnt)])
            carryV[0] = p.v(lambda v: (v.tensor_tensor(
                carry[:, :], carry[:, :], av["ps2_s"][1][0:16, 8:16], op=OP.add),
                v.drain())[-1])
        stat_rd[0] = p.c("V")

    if DBG_PHASE == 2:
        allw2 = [("V", p.c("V")), ("T", p.c("T")), ("Sc", p.c("Sc"))]
        p.dma("gpsimd", "Cdbg", lambda g: g.dma_start(out=A["dbg_wot"], in_=av["woT"][:, :, :]),
              waits=allw2)
        p.dma("gpsimd", "Cdbg", lambda g: g.dma_start(out=A["dbg_q2f"], in_=av["q2f"][:, :, :]))
        p.dma("gpsimd", "Cdbg", lambda g: g.dma_start(out=A["dbg_gf"], in_=c2["gf"][:, :]))
        p.dma("gpsimd", "Cdbg", lambda g: g.dma_start(out=A["dbg_wgh"], in_=c2["wgh"][:, :, :]))
        p.dma("gpsimd", "Cdbg", lambda g: g.dma_start(out=A["dbg_x1f"], in_=av["x1f"][:, :, :]))
    M["p2a_T"] = p.c("T")
    M["carry_done"] = carryV[0]
    M["cntev"] = dict(cntev)



def ln_chain(p, c2, mSQ):
    p.v(lambda v: (v.tensor_scalar(c2["meanb"][:, :], c2["s1b"][:, :], 1.0 / C, None,
                                   op0=OP.mult), v.drain())[-1])
    p.v(lambda v: (v.tensor_tensor(c2["m2tb"][:, :], c2["meanb"][:, :], c2["meanb"][:, :],
                                   op=OP.mult), v.drain())[-1])
    p.v(lambda v: (v.scalar_tensor_tensor(
        c2["varb"][:, :], c2["s2b"][:, :], 1.0 / C, c2["m2tb"][:, :],
        op0=OP.mult, op1=OP.subtract), v.drain())[-1], waits=[("Sc", mSQ)])
    mv = p.v(lambda v: (v.tensor_scalar(c2["varb"][:, :], c2["varb"][:, :], EPS, None,
                                        op0=OP.add), v.drain())[-1])
    ms = p.s(lambda sc: (sc.activation(c2["rstdb"][:, :], c2["varb"][:, :], AF.Sqrt,
                                       scale=1.0), sc.drain())[-1], waits=[("V", mv)])
    p.v(lambda v: (v.reciprocal(c2["rstdb"][:, :], c2["rstdb"][:, :]), v.drain())[-1],
        waits=[("Sc", ms)])
    p.v(lambda v: (v.tensor_tensor(c2["m2tb"][:, :], c2["rstdb"][:, :], c2["rstdb"][:, :],
                                   op=OP.mult), v.drain())[-1])
    p.v(lambda v: (v.tensor_tensor(c2["m2tb"][:, :], c2["m2tb"][:, :], c2["varb"][:, :],
                                   op=OP.mult), v.drain())[-1])
    p.v(lambda v: (v.tensor_scalar(c2["m2tb"][:, :], c2["m2tb"][:, :], -0.5, 1.5,
                                   op0=OP.mult, op1=OP.add), v.drain())[-1])
    return p.v(lambda v: (v.tensor_tensor(c2["rstdb"][:, :], c2["rstdb"][:, :],
                                          c2["m2tb"][:, :], op=OP.mult), v.drain())[-1])


def gen_route_finish(p, A, cst, c2, bv2, drs, M):
    p.v(lambda v: v.memset(c2["coefs"][:, :], 1.0))
    cnts, gval, ohb = c2["cnts"], c2["gval"], c2["ohb"]
    coefm, coefs, carry, offsel = c2["coefm"], c2["coefs"], c2["carry"], c2["offsel"]
    CAP5 = CAPACITY + 0.5
    mcar = p.dma("gpsimd", "Cct", lambda g: g.dma_start(
        out=drs["cnt_in"].ap(), in_=carry[:, :]), waits=[("V", M["carry_done"])])
    p.ev("gpsimd", lambda g: g.collective_compute(
        "AllGather", OP.bypass, replica_groups=[list(range(NCORES))],
        ins=[drs["cnt_in"].ap().opt()], outs=[drs["cnt_ag"].ap().opt()]),
        waits=[("Cct", mcar)], inc=("CC", 1))
    magc = p.c("CC")
    mcw = p.dma("gpsimd", "Cct", lambda g: g.dma_start(
        out=c2["cw"][:, :, :], in_=drs["cnt_ag"].ap().transpose([1, 2, 0]),
        allow_slow_non_contiguous=True),
        waits=[("CC", magc)])
    p.v(lambda v: (v.tensor_tensor(c2["cw2"][:, :, :], c2["cw"][:, :, :],
                                   c2["rke"][:, :, :], op=OP.mult), v.drain())[-1],
        waits=[("Cct", mcw)])
    p.v(lambda v: (v.tensor_reduce(c2["rk16"][:, :], c2["cw2"][:, :, :], axis=AX,
                                   op=OP.add), v.drain())[-1])
    mtot = p.v(lambda v: (v.tensor_reduce(c2["tot16"][:, :], c2["cw"][:, :, :], axis=AX,
                                          op=OP.add), v.drain())[-1])
    p.t(lambda t: t.matmul(bv2["zps"][0][0:16, 0:8], c2["btri"][:, :], c2["tot16"][:, :],
                           start=True, stop=True), waits=[("V", mtot)])
    mob = p.c("T")
    moff = p.v(lambda v: (v.tensor_tensor(c2["off16"][:, :], bv2["zps"][0][0:16, 0:8],
                                          c2["rk16"][:, :], op=OP.add), v.drain())[-1],
               waits=[("T", mob)])
    p.t(lambda t: t.matmul(bv2["zps"][1][:, 0:8], c2["bpat"][:, :], c2["off16"][:, :],
                           start=True, stop=True), waits=[("V", moff)])
    mos = p.c("T")
    mosel = p.v(lambda v: (v.tensor_copy(offsel[:, :], bv2["zps"][1][:, 0:8]),
                           v.drain())[-1], waits=[("T", mos)])
    trev = [0]
    for blkg in range(32):
        p.v(lambda v, blkg=blkg: (v.tensor_tensor(
            bv2["rtb"][:, :], cnts[:, blkg, :], offsel[:, :], op=OP.add), v.drain())[-1],
            waits=[("V", mosel), ("V", M["cntev"][blkg])])
        p.v(lambda v: (v.tensor_scalar(bv2["rtb"][:, :], bv2["rtb"][:, :], CAP5, None,
                                       op0=OP.is_lt), v.drain())[-1])
        p.v(lambda v, blkg=blkg: (v.tensor_tensor(
            bv2["rtb"][:, :], bv2["rtb"][:, :], ohb[:, blkg, :], op=OP.mult), v.drain())[-1])
        mcoef = p.v(lambda v, blkg=blkg: (v.tensor_scalar(
            coefm[:, blkg, :], bv2["rtb"][:, :], gval[:, blkg:blkg + 1], None,
            op0=OP.mult), v.drain())[-1])
        p.t(lambda t, blkg=blkg: t.transpose(
            bv2["zps"][2][0:8, 0:128], coefm[:, blkg, :], c2["idf"][:, :]),
            waits=[("V", mcoef), ("V", trev[0])])
        mtr = p.c("T")
        trev[0] = p.v(lambda v, blkg=blkg: (v.tensor_copy(
            coefs[0:8, 128 * blkg:128 * (blkg + 1)], bv2["zps"][2][0:8, 0:128]),
            v.drain())[-1], waits=[("T", mtr)])
    M["coef_done"] = p.c("V")
    if DBG_PHASE == 2:
        allw = [("V", p.c("V")), ("T", p.c("T")), ("Sc", p.c("Sc"))]
        p.dma("gpsimd", "Cdbg", lambda g: g.dma_start(
            out=A["dbg_a2o"], in_=drs["a2a_out"].ap()), waits=[("CC", M["a2a_done"])])
        p.dma("gpsimd", "Cdbg", lambda g: g.dma_start(
            out=A["dbg_wit"], in_=drs["wiT_sp"].ap()))
        for nm, tile in [("dbg_y1f", c2["y1f"]), ("dbg_cnts", cnts), ("dbg_gval", gval),
                         ("dbg_ohb", ohb), ("dbg_coefm", coefm), ("dbg_offsel", offsel)]:
            p.dma("gpsimd", "Cdbg", lambda g, nm=nm, tile=tile: g.dma_start(
                out=A[nm], in_=tile[:, :] if nm in ("dbg_gval", "dbg_offsel") else tile[:, :, :]),
                waits=allw)




def gen_phase2bc(p, A, cst, c2, bv2, drs, M):
    ones = cst["ones"]
    y1f, coefm, coefs = c2["y1f"], c2["coefm"], c2["coefs"]
    wc, ztile, y1tok, h2 = bv2["wc"], bv2["ztile"], bv2["y1tok"], bv2["h2"]
    x1rl, outtok, zps = bv2["x1rl"], bv2["outtok"], bv2["zps"]
    a2a_out, wcs, out = drs["a2a_out"], drs["wcs"], drs["out"]

    zT = {}
    zev = [0, 0, 0, 0]
    ytb_free = [0, 0]
    stat_rd = [0]
    x1r_free = [0, 0]
    out_cnt = [0, 0]
    widx = [0]
    ztile_free = [0, 0]
    h2_free = [0]
    zsl = 0

    for tau in range(8):
        zs = (tau % 2) * 4
        for e in range(E):
            i = widx[0]
            chz = f"Cze{i % 2}"
            mw = p.dma("gpsimd", chz, lambda g, e=e, i=i: g.dma_start(
                out=wc[:, i % 2, :, :], in_=wcs.ap()[e].rearrange("(k p) d -> p k d", p=128)),
                waits=[("CC", M["ag_done"]), ("T", zT.get(i - 2, 0))])
            widx[0] += 1
            for tb in range(4):
                blkg = 4 * tau + tb
                for chp in range(2):
                    pz = (tb % 2) * 2 + chp
                    for cc in range(8):
                        w = []
                        if cc == 0:
                            w = [(chz, mw), ("V", M[("y1f_ready", tau)]),
                                 ("V", M["coef_done"]), ("V", zev[pz])]
                            if e == 0:
                                w.append(("V", ztile_free[tau % 2]))
                        p.t(lambda t, cc=cc, blkg=blkg, chp=chp, i=i, pz=pz: t.matmul(
                            zps[pz][:, :], y1f[:, cc, 128 * blkg:128 * (blkg + 1)],
                            wc[:, i % 2, cc, 512 * chp:512 * (chp + 1)],
                            start=(cc == 0), stop=(cc == 7)), waits=w)
                    mz = p.c("T")
                    if e == 0:
                        zev[pz] = p.v(lambda v, tb=tb, chp=chp, blkg=blkg, e=e, pz=pz, zs=zs: v.tensor_scalar(
                            ztile[:, zs + tb, 512 * chp:512 * (chp + 1)], zps[pz][:, :],
                            coefm[:, blkg, e:e + 1], None, op0=OP.mult),
                            waits=[("T", mz)])
                    else:
                        zev[pz] = p.v(lambda v, tb=tb, chp=chp, blkg=blkg, e=e, pz=pz, zs=zs: v.scalar_tensor_tensor(
                            ztile[:, zs + tb, 512 * chp:512 * (chp + 1)], zps[pz][:, :],
                            coefm[:, blkg, e:e + 1],
                            ztile[:, zs + tb, 512 * chp:512 * (chp + 1)],
                            op0=OP.mult, op1=OP.add), waits=[("T", mz)])
            zT[i] = p.c("T")
        for tb in range(4):
            blkg = 4 * tau + tb
            for chp in range(2):
                pz = (tb % 2) * 2 + chp
                p.t(lambda t, blkg=blkg, chp=chp, pz=pz: t.matmul(
                    zps[pz][:, :], coefs[0:9, 128 * blkg:128 * (blkg + 1)],
                    c2["biasm"][0:9, 512 * chp:512 * (chp + 1)], start=True, stop=True),
                    waits=[("V", zev[pz]), ("V", M["coef_done"]), ("Ccg", M["mCG"])])
                mz = p.c("T")
                zev[pz] = p.v(lambda v, tb=tb, chp=chp, pz=pz, zs=zs: v.tensor_tensor(
                    ztile[:, zs + tb, 512 * chp:512 * (chp + 1)], zps[pz][:, :],
                    ztile[:, zs + tb, 512 * chp:512 * (chp + 1)], op=OP.add),
                    waits=[("T", mz)])
            chy = f"Cyh{tb % 2}"
            for j in range(8):
                p.dma("sync", chy, lambda y, blkg=blkg, tb=tb, j=j: y.dma_start(
                    out=y1tok[:, tb % 2, 128 * j:128 * (j + 1)],
                    in_=y1f[:, j, 128 * blkg:128 * (blkg + 1)], transpose=True),
                    waits=[("V", ytb_free[tb % 2])])
            mytb = p.c(chy)
            mr3 = p.v(lambda v, tb=tb, zs=zs: v.tensor_tensor(
                ztile[:, zs + tb, :], y1tok[:, tb % 2, :], ztile[:, zs + tb, :], op=OP.add),
                waits=[(chy, mytb)])
            ytb_free[tb % 2] = mr3
            p.v(lambda v, tb=tb, zs=zs: (v.tensor_reduce(
                c2["s1b"][:, tb:tb + 1], ztile[:, zs + tb, :], axis=AX, op=OP.add),
                v.drain())[-1], waits=[("V", stat_rd[0])])
            p.s(lambda sc, tb=tb, zs=zs: (sc.activation(
                c2["junk2"][:, :], ztile[:, zs + tb, :], AF.Square,
                accum_out=c2["s2b"][:, tb:tb + 1]), sc.drain())[-1],
                waits=[("V", mr3), ("V", stat_rd[0])])
        mSQ = p.c("Sc")
        ln_chain(p, c2, mSQ)
        for tb in range(4):
            mh = p.v(lambda v, tb=tb, zs=zs: v.tensor_scalar(
                h2[:, tb, :], ztile[:, zs + tb, :], c2["meanb"][:, tb:tb + 1],
                c2["rstdb"][:, tb:tb + 1], op0=OP.subtract, op1=OP.mult),
                waits=[("V", h2_free[0])])
            mg1 = p.g(lambda g, tb=tb: g.tensor_tensor(
                h2[:, tb, :], h2[:, tb, :], c2["g2b"][:, :], op=OP.mult),
                waits=[("V", mh)])
            mg2 = p.g(lambda g, tb=tb: g.tensor_tensor(
                h2[:, tb, :], h2[:, tb, :], c2["b2b"][:, :], op=OP.add))
            chr_ = f"Cx1r{tb % 2}"
            mxr = p.dma("gpsimd", chr_, lambda g, tau=tau, tb=tb: g.dma_start(
                out=x1rl[:, tb % 2, :],
                in_=a2a_out.ap()[:, :, 32 * tau + 8 * tb:32 * tau + 8 * (tb + 1), :]
                .transpose([2, 0, 1, 3])),
                waits=[("V", x1r_free[tb % 2])])
            mr4 = p.v(lambda v, tb=tb: v.tensor_tensor(
                h2[:, tb, :], x1rl[:, tb % 2, :], h2[:, tb, :], op=OP.add),
                waits=[("G", mg2), (chr_, mxr)])
            x1r_free[tb % 2] = mr4
            p.v(lambda v, tb=tb: (v.tensor_reduce(
                c2["s1b"][:, tb:tb + 1], h2[:, tb, :], axis=AX, op=OP.add),
                v.drain())[-1])
            p.s(lambda sc, tb=tb: (sc.activation(
                c2["junk2"][:, :], h2[:, tb, :], AF.Square,
                accum_out=c2["s2b"][:, tb:tb + 1]), sc.drain())[-1],
                waits=[("V", mr4)])
        ztile_free[tau % 2] = p.c("V")
        mSQ = p.c("Sc")
        stat_rd[0] = ln_chain(p, c2, mSQ)
        for tb in range(4):
            cho = f"Cout{tb % 2}"
            mo = p.v(lambda v, tb=tb: v.tensor_scalar(
                outtok[:, tb % 2, :], h2[:, tb, :], c2["meanb"][:, tb:tb + 1],
                c2["rstdb"][:, tb:tb + 1], op0=OP.subtract, op1=OP.mult),
                waits=[(cho, 16 * out_cnt[tb % 2])])
            n0 = 32 * tau + 8 * tb
            p.dma("scalar", cho, lambda sc, n0=n0, tb=tb: sc.dma_start(
                out=out[:, n0:n0 + 8, :].transpose([1, 0, 2]),
                in_=outtok[:, tb % 2, :]), waits=[("V", mo)])
            out_cnt[tb % 2] += 1
        h2_free[0] = p.c("V")
    p.ev("gpsimd", None, waits=[("Cout0", 16 * out_cnt[0]), ("Cout1", 16 * out_cnt[1])])
    p.ev("scalar", None, waits=[("Cout0", 16 * out_cnt[0]), ("Cout1", 16 * out_cnt[1])])

_BUILT = None


def kernel(**inputs):
    global _BUILT
    core_inputs = []
    x = np.ascontiguousarray(inputs["L0_attn_x"])
    for c in range(NCORES):
        d = host_constants(c)
        d.update(dict(
            x=x[2 * c:2 * c + 2],
            w_attn=inputs["L0_attn_w_attn"], b_attn=inputs["L0_attn_b_attn"],
            w_proj=inputs["L0_attn_w_proj"], b_proj=inputs["L0_attn_b_proj"],
            in_w=inputs["L0_ffn_in_proj_w"], in_b=inputs["L0_ffn_in_proj_b"],
            out_w=inputs["L0_ffn_out_proj_w"], out_b=inputs["L0_ffn_out_proj_b"],
            ln1_g=inputs["L0_ffn_ln1_g"], ln1_b=inputs["L0_ffn_ln1_b"],
            ln2_g=inputs["L0_ffn_ln2_g"], ln2_b=inputs["L0_ffn_ln2_b"],
            wg=inputs["L0_ffn_wg"],
            w1_own=inputs["L0_ffn_w1"][c], b1_own=inputs["L0_ffn_b1"][c],
            w2_own=inputs["L0_ffn_w2"][c], b2_own=inputs["L0_ffn_b2"][c],
        ))
        core_inputs.append({k: np.ascontiguousarray(v) for k, v in d.items()})

    with allow_ici_all_to_all():
        if _BUILT is None:
            _BUILT = build_nc()
        res = run_bass_kernel_spmd(_BUILT, core_inputs, core_ids=list(range(NCORES)))

    if DBG_PHASE == 1:
        return [res.results[c]["dbg_x1"] for c in range(NCORES)]
    if DBG_PHASE == 2:
        return res
    outp = np.empty((B, T, C), dtype=np.float32)
    for c in range(NCORES):
        outp[:, NSL * c:NSL * (c + 1), :] = res.results[c]["out"]
    return outp
